# version 10
# baseline (speedup 1.0000x reference)
import numpy as np
import ml_dtypes
import concourse.bass as bass
import concourse.mybir as mybir
from concourse.bass_utils import run_bass_kernel_spmd

F32 = mybir.dt.float32
BF16 = mybir.dt.bfloat16
AF = mybir.ActivationFunctionType
ALU = mybir.AluOpType
BF = ml_dtypes.bfloat16

D = 4096
NCH = 32
SC = 1024
NST = 2
SCALE_ATTN = float(1.0 / np.sqrt(128.0))
SCALE_LORA = float(1.0 / np.sqrt(32.0))
INV_D = float(1.0 / 4096.0)

OWN = [[0, 3, 4, 7, 8, 11, 12, 15], [1, 2, 5, 6, 9, 10, 13, 14]]
PERM_ABS = [0, 1, 3, 2, 4, 5, 7, 6, 8, 9, 11, 10, 12, 13, 15, 14]

T1B = 32768
KVB = 49152
VTB = 57344
XBB = 0
SQB = 4096
LQ0 = 6144; LQ1 = 6656; LK0 = 7168; LK1 = 7680
LP0 = 8192; LP1 = 8704; LP8 = 9216; LH = 9728; LHW = 10240
HW2 = 10752
VSTC = 11776
RM = 0; RX = 512; RT = 1024


def build_nc(debug=False, upto=9):
    nc = bass.Bass("TRN2", target_bir_lowering=False, debug=False)

    xd = nc.dram_tensor("x", [128, NCH * SC], F32, kind="ExternalInput")
    wffd = nc.dram_tensor("wff", [128, 32 * 4096], F32, kind="ExternalInput")
    wqkvd = nc.dram_tensor("wqkv", [128, 48 * 4096], F32, kind="ExternalInput")
    wdd = nc.dram_tensor("wd", [128, 32 * 4096], F32, kind="ExternalInput")
    wqd = nc.dram_tensor("wq", [128, 32 * 256], F32, kind="ExternalInput")
    wkd = nc.dram_tensor("wk", [128, 32 * 256], F32, kind="ExternalInput")
    atd = nc.dram_tensor("at", [128, 32 * 128], F32, kind="ExternalInput")
    bm2d = nc.dram_tensor("bm2", [128, 4096], F32, kind="ExternalInput")
    b2d = nc.dram_tensor("b2", [128, 32], F32, kind="ExternalInput")
    cbd = nc.dram_tensor("cb", [128, 512], BF16, kind="ExternalInput")
    cfd = nc.dram_tensor("cf", [128, 132], F32, kind="ExternalInput")
    maskd = nc.dram_tensor("mask", [128, 2048], BF16, kind="ExternalInput")

    okind = dict(kind="ExternalOutput") if debug else {}
    x1d = nc.dram_tensor("x1d", [128, NCH * SC], F32, **okind)
    x2d = nc.dram_tensor("x2d", [128, NCH * SC], F32, **okind)
    kvpay = nc.dram_tensor("kvpay", [2048, 1024], BF16)
    kvgath = nc.dram_tensor("kvgath", [4096, 1024], BF16)
    outd = nc.dram_tensor("out", [128, NCH * SC], F32, kind="ExternalOutput")
    if debug:
        qadump = nc.dram_tensor("qadump", [128, 32768], BF16, kind="ExternalOutput")

    import contextlib
    with contextlib.ExitStack() as stack:
        en = stack.enter_context
        AB = en(nc.sbuf_tensor("AB", [128, 65536], BF16))
        WSL = en(nc.sbuf_tensor("WSL", [128, 8192], BF16))
        BM2S = en(nc.sbuf_tensor("BM2S", [128, 4096], BF16))
        SB2 = en(nc.sbuf_tensor("SB2", [128, 12288], BF16))
        PT = en(nc.sbuf_tensor("PTS", [128, 4096], BF16))
        MSK = en(nc.sbuf_tensor("MSK", [128, 2048], BF16))
        CBS = en(nc.sbuf_tensor("CBS", [128, 512], BF16))
        CFS = en(nc.sbuf_tensor("CFS", [128, 132], F32))
        HB = en(nc.sbuf_tensor("HB", [128, 32], F32))
        XS = en(nc.sbuf_tensor("XS", [128, 1024], F32))
        US = en(nc.sbuf_tensor("US", [128, 1024], F32))
        U2 = en(nc.sbuf_tensor("U2", [128, 1024], F32))
        ROWS = en(nc.sbuf_tensor("ROWS", [1, 1536], F32))

        P01 = en(nc.psum_tensor("P01", [128, 1024], F32))
        P23 = en(nc.psum_tensor("P23", [128, 1024], F32))
        P45 = en(nc.psum_tensor("P45", [128, 1024], F32))
        PG = en(nc.psum_tensor("PG", [128, 512], F32))
        PT8 = en(nc.psum_tensor("PT8", [128, 1024], BF16))
        PA = P01[:, 0:512]
        PB = P01[:, 512:1024]
        PC = P23[:, 0:512]
        PD = P23[:, 512:1024]
        PE_ = P45[:, 0:512]
        PF = P45[:, 512:1024]

        SEM_NAMES = ("SP", "SA", "SD", "SCC", "GC", "GW0", "GW1", "GW2", "GL", "GL1", "GX0", "GX1",
                     "YC", "YX0", "YX1", "YW0", "YW1", "YP", "YG", "AC0")
        SEM = {n: en(nc.semaphore(n)) for n in SEM_NAMES}
        ops = {k: [] for k in ("t", "s", "v", "g", "y")}
        cnt = {n: 0 for n in SEM_NAMES}

        def wt(eng, sname, val):
            if val > 0:
                ops[eng].append(lambda e, _s=SEM[sname], _v=val: e.wait_ge(_s, _v))

        def run(eng, builder, inc=None, n=1):
            if inc is None:
                ops[eng].append(builder)
            else:
                bump(inc, n)
                ops[eng].append(lambda e, _b=builder, _s=SEM[inc], _n=n: _b(e).then_inc(_s, _n))

        def bump(sname, n=1):
            cnt[sname] += n
            return cnt[sname]

        def dma(eng, sname, out_ap, in_ap):
            run(eng, lambda e, o=out_ap, i=in_ap: e.dma_start(out=o, in_=i), sname, 16)
            return cnt[sname]

        led = {}

        def set_led(key, sname, val):
            led[key] = (sname, val)

        def wait_led(eng, key):
            if key in led:
                s, v = led[key]
                wt(eng, s, v)

        def run_ser(eng, sem, builder):
            wt(eng, sem, cnt[sem])
            run(eng, builder, sem)

        ONES = CBS[:, 0:1]
        ONES8 = CBS[0:8, 0:1]
        SEL0 = CBS[:, 1:9]
        SEL1 = CBS[:, 9:17]
        EXP8 = CBS[0:8, 17:145]
        IDEN = CBS[:, 145:273]
        ONEROW = CFS[0:1, 0:128]
        ONEROWB = CBS[0:1, 273:401]
        EPS = CFS[0:1, 128:129]
        M_ROW = ROWS[0:1, RM:RM + 512]
        X_ROW = ROWS[0:1, RX:RX + 512]
        T_ROW = ROWS[0:1, RT:RT + 512]

        dma("g", "GC", CBS[:, :], cbd[:, :])
        dma("y", "YC", CFS[:, :], cfd[:, :])
        dma("g", "GC", MSK[:, :], maskd[:, :])
        dma("y", "YC", HB[:, :], b2d[:, :])
        dma("g", "GC", BM2S[:, :], bm2d[:, :])
        v_consts_g = cnt["GC"]
        v_consts_x = cnt["YC"]
        wt("v", "YC", v_consts_x)
        run("v", lambda e: e.tensor_scalar_mul(HB[:, :], HB[:, :], 0.5), "SD")
        v_hb = cnt["SD"]
        wt("t", "GC", v_consts_g)
        wt("s", "SD", v_hb)
        wt("s", "YC", v_consts_x)
        wt("v", "GC", v_consts_g)

        def emit_ln(src, s_off, tdest, g_hook=None):
            src3 = src.ap().rearrange("p (c s) -> p c s", s=SC)
            pe_stats = {}
            for grp in range(8):
                buf = grp % 2
                if grp < 2:
                    wait_led("g", "xb")
                    wait_led("g", "xb2")
                else:
                    wt("g", "SP", pe_stats[grp - 2])
                xbv = SB2[:, XBB + buf * 2048: XBB + (buf + 1) * 2048].rearrange(
                    "p (c s) -> p c s", s=512)
                xch = "GX%d" % buf
                wt("g", xch, cnt[xch])
                vload = dma("g", xch, xbv, src3[:, grp * 4:(grp + 1) * 4, s_off:s_off + 512])
                wt("v", xch, vload)
                if grp - 1 in pe_stats:
                    wt("v", "SP", pe_stats[grp - 1])
                elif grp == 0:
                    wait_led("v", "sq")
                for i in range(4):
                    xb_i = SB2[:, XBB + buf * 2048 + i * 512: XBB + buf * 2048 + (i + 1) * 512]
                    sq_i = SB2[:, SQB + i * 512: SQB + (i + 1) * 512]
                    run("v", lambda e, a=sq_i, b=xb_i: e.tensor_tensor(a, b, b, ALU.mult),
                        "SD" if i == 3 else None)
                v_sq = cnt["SD"]
                wt("t", xch, vload)
                if grp == 0:
                    wait_led("t", "bank_PG")
                    wait_led("t", "bank_PC")
                for i in range(4):
                    xb_i = SB2[:, XBB + buf * 2048 + i * 512: XBB + buf * 2048 + (i + 1) * 512]
                    st0 = (grp == 0 and i == 0)
                    sp1 = (grp == 7 and i == 3)
                    run("t", lambda e, r=xb_i, s0=st0, s1=sp1: e.matmul(
                        PG[0:1, :], ONES, r, start=s0, stop=s1))
                wt("t", "SD", v_sq)
                for i in range(4):
                    sq_i = SB2[:, SQB + i * 512: SQB + (i + 1) * 512]
                    st0 = (grp == 0 and i == 0)
                    sp1 = (grp == 7 and i == 3)
                    run("t", lambda e, r=sq_i, s0=st0, s1=sp1: e.matmul(
                        PC[0:1, :], ONES, r, start=s0, stop=s1),
                        "SP" if i == 3 else None)
                pe_stats[grp] = cnt["SP"]
            wt("v", "SP", pe_stats[7])
            wait_led("v", "rows")
            run_ser("v", "SD", lambda e: e.tensor_scalar_mul(M_ROW, PG[0:1, :], INV_D))
            run_ser("v", "SD", lambda e: e.tensor_tensor(T_ROW, M_ROW, M_ROW, ALU.mult))
            run_ser("v", "SD", lambda e: e.scalar_tensor_tensor(
                X_ROW, PC[0:1, :], INV_D, T_ROW, ALU.mult, ALU.subtract))
            v_var = cnt["SD"]
            set_led("bank_PG", "SD", v_var)
            set_led("bank_PC", "SD", v_var)
            wt("s", "SD", v_var)
            run_ser("s", "SA", lambda e: e.activation(T_ROW, X_ROW, AF.Ln, bias=EPS, scale=1.0))
            run_ser("s", "SA", lambda e: e.activation(X_ROW, T_ROW, AF.Exp, scale=-0.5))
            v_rstd = cnt["SA"]
            wt("v", "SA", v_rstd)
            run("v", lambda e: e.scalar_tensor_tensor(
                T_ROW, M_ROW, -1.0, X_ROW, ALU.mult, ALU.mult), "SD")
            v_negmr = cnt["SD"]
            wt("t", "SD", v_negmr)
            wait_led("t", "bank_PE_")
            wait_led("t", "bank_PF")
            run("t", lambda e: e.matmul(PE_[:, :], ONEROW, X_ROW, start=True, stop=True))
            run("t", lambda e: e.matmul(PF[:, :], ONEROW, T_ROW, start=True, stop=True), "SP")
            v_bc = cnt["SP"]
            set_led("rows", "SP", v_bc)
            if g_hook is not None:
                g_hook()
            v_apply = {}
            for grp in range(8):
                buf = grp % 2
                if grp < 2:
                    wt("g", "SP", pe_stats[7])
                else:
                    wt("g", "SD", v_apply[grp - 2])
                xbv = SB2[:, XBB + buf * 2048: XBB + (buf + 1) * 2048].rearrange(
                    "p (c s) -> p c s", s=512)
                xch = "GX%d" % buf
                wt("g", xch, cnt[xch])
                vload = dma("g", xch, xbv, src3[:, grp * 4:(grp + 1) * 4, s_off:s_off + 512])
                wt("v", xch, vload)
                if grp == 0:
                    wt("v", "SP", v_bc)
                for i in range(4):
                    ch = grp * 4 + i
                    xb_i = SB2[:, XBB + buf * 2048 + i * 512: XBB + buf * 2048 + (i + 1) * 512]
                    tmp_i = SB2[:, SQB + i * 512: SQB + (i + 1) * 512]
                    tch = AB[:, tdest + ch * 512: tdest + (ch + 1) * 512]
                    run_ser("v", "SD", lambda e, a=tmp_i, b=xb_i:
                            e.tensor_tensor(a, b, PF[:, :], ALU.add))
                    run_ser("v", "SD", lambda e, a=tch, b=tmp_i:
                            e.tensor_tensor(a, b, PE_[:, :], ALU.mult))
                v_apply[grp] = cnt["SD"]
            set_led("bank_PE_", "SD", v_apply[7])
            set_led("bank_PF", "SD", v_apply[7])
            set_led("xb", "SD", v_apply[7])
            set_led("sq", "SD", v_apply[7])
            return v_apply[7]

        def emit_lora_loads(pref):
            wait_led("g", "WSL0")
            wait_led("g", "WSL1")
            wt("g", "GL", cnt["GL"])
            pref["wq0"] = dma("g", "GL", WSL[:, 0:4096], wqd[:, 0:4096])
            wt("g", "GL1", cnt["GL1"])
            pref["wq1"] = dma("g", "GL1", WSL[:, 4096:8192], wqd[:, 4096:8192])

        def emit_lora(st, tbase, v_apply, pref):
            def t_ch(ch):
                return AB[:, tbase + ch * 512: tbase + (ch + 1) * 512]

            wt("t", "GL", pref["wq0"])
            wt("t", "SD", v_apply)
            wait_led("t", "bank_PA")
            wait_led("t", "bank_PB")
            v_qh1 = 0
            for ich in range(32):
                if ich == 16:
                    wt("t", "GL1", pref["wq1"])
                for ch2, bank in ((0, PA), (1, PB)):
                    lh = WSL[:, ich * 256 + ch2 * 128: ich * 256 + (ch2 + 1) * 128]
                    run("t", lambda e, l=lh, r=t_ch(ich), s0=(ich == 0), s1=(ich == 31),
                        b=bank: e.matmul(b[:, :], l, r, start=s0, stop=s1),
                        "SP" if (ich in (15, 31) and ch2 == 1) else None)
                if ich == 15:
                    v_qh1 = cnt["SP"]
            v_qmm = cnt["SP"]
            wt("g", "SP", v_qh1)
            wt("g", "GL", cnt["GL"])
            wk0 = dma("g", "GL", WSL[:, 0:4096], wkd[:, 0:4096])
            wt("g", "SP", v_qmm)
            wt("g", "GL1", cnt["GL1"])
            wk1 = dma("g", "GL1", WSL[:, 4096:8192], wkd[:, 4096:8192])
            wt("s", "SP", v_qmm)
            run("s", lambda e: e.activation(SB2[:, LQ0:LQ0 + 512], PA, AF.Copy))
            run("s", lambda e: e.activation(SB2[:, LQ1:LQ1 + 512], PB, AF.Copy), "SA")
            v_qe = cnt["SA"]
            set_led("bank_PA", "SA", v_qe)
            set_led("bank_PB", "SA", v_qe)
            wt("t", "GL", wk0)
            wait_led("t", "bank_PC")
            wait_led("t", "bank_PD")
            v_kh1 = 0
            for ich in range(32):
                if ich == 16:
                    wt("t", "GL1", wk1)
                for ch2, bank in ((0, PC), (1, PD)):
                    lh = WSL[:, ich * 256 + ch2 * 128: ich * 256 + (ch2 + 1) * 128]
                    run("t", lambda e, l=lh, r=t_ch(ich), s0=(ich == 0), s1=(ich == 31),
                        b=bank: e.matmul(b[:, :], l, r, start=s0, stop=s1),
                        "SP" if (ich in (15, 31) and ch2 == 1) else None)
                if ich == 15:
                    v_kh1 = cnt["SP"]
            v_kmm = cnt["SP"]
            wt("g", "SP", v_kh1)
            wt("g", "GL", cnt["GL"])
            v_at = dma("g", "GL", WSL[:, 0:4096], atd[:, :])
            wt("s", "SP", v_kmm)
            run("s", lambda e: e.activation(SB2[:, LK0:LK0 + 512], PC, AF.Copy))
            run("s", lambda e: e.activation(SB2[:, LK1:LK1 + 512], PD, AF.Copy), "SA")
            v_ke = cnt["SA"]
            set_led("bank_PC", "SA", v_ke)
            set_led("bank_PD", "SA", v_ke)
            wt("v", "SA", v_ke)
            run("v", lambda e: e.tensor_tensor(SB2[:, LP0:LP0 + 512], SB2[:, LQ0:LQ0 + 512],
                                               SB2[:, LK0:LK0 + 512], ALU.mult))
            run("v", lambda e: e.tensor_tensor(SB2[:, LP1:LP1 + 512], SB2[:, LQ1:LQ1 + 512],
                                               SB2[:, LK1:LK1 + 512], ALU.mult), "SD")
            v_prod = cnt["SD"]
            wt("t", "SD", v_prod)
            wait_led("t", "bank_PG")
            run("t", lambda e: e.matmul(PG[0:8, :], SEL0, SB2[:, LP0:LP0 + 512],
                                        start=True, stop=False))
            run("t", lambda e: e.matmul(PG[0:8, :], SEL1, SB2[:, LP1:LP1 + 512],
                                        start=False, stop=True), "SP")
            v_s8 = cnt["SP"]
            wt("s", "SP", v_s8)
            run("s", lambda e: e.activation(SB2[0:8, LP8:LP8 + 512], PG[0:8, :], AF.Exp,
                                            scale=SCALE_LORA), "SA")
            v_p8 = cnt["SA"]
            wt("t", "SA", v_p8)
            run("t", lambda e: e.matmul(PG[64:65, :], ONES8, SB2[0:8, LP8:LP8 + 512],
                                        start=True, stop=True), "SP")
            v_z8 = cnt["SP"]
            wt("v", "SP", v_z8)
            wait_led("v", "rows")
            run("v", lambda e: e.reciprocal(M_ROW, PG[64:65, :]), "SD")
            v_zi = cnt["SD"]
            set_led("bank_PG", "SD", v_zi)
            wt("t", "SD", v_zi)
            wait_led("t", "bank_PE_")
            wait_led("t", "bank_PF")
            run("t", lambda e: e.matmul(PE_[:, :], EXP8, SB2[0:8, LP8:LP8 + 512],
                                        start=True, stop=True))
            run("t", lambda e: e.matmul(PF[:, :], ONEROW, M_ROW, start=True, stop=True), "SP")
            v_bc8 = cnt["SP"]
            set_led("rows", "SP", v_bc8)
            wt("t", "GL", v_at)
            for ich in range(32):
                lh = WSL[:, ich * 128:(ich + 1) * 128]
                run("t", lambda e, l=lh, r=t_ch(ich), s0=(ich == 0), s1=(ich == 31):
                    e.matmul(PA, l, r, start=s0, stop=s1),
                    "SP" if ich == 31 else None)
            v_h = cnt["SP"]
            set_led("WSL0", "SP", v_h)
            set_led("WSL1", "SP", v_kmm)
            wt("s", "SP", v_h)
            run("s", lambda e: e.activation(SB2[:, LH:LH + 512], PA, AF.Copy), "SA")
            v_he = cnt["SA"]
            set_led("bank_PA", "SA", v_he)
            wt("v", "SA", v_he)
            wt("v", "SP", v_bc8)
            run_ser("v", "SD", lambda e: e.tensor_tensor(SB2[:, LHW:LHW + 512], PE_[:, :],
                                                         SB2[:, LH:LH + 512], ALU.mult))
            hw2c = HW2 + st * 512
            run_ser("v", "SD", lambda e, c=hw2c: e.tensor_tensor(
                SB2[:, c:c + 512], PF[:, :], SB2[:, LHW:LHW + 512], ALU.mult))
            v_hw2 = cnt["SD"]
            set_led("bank_PE_", "SD", v_hw2)
            set_led("bank_PF", "SD", v_hw2)
            return v_hw2

        def emit_oloop(wsrc, rhs_col, resid_src, dest, ffn=True,
                       pre_waits=(), xs_raw=()):
            n_oc = 32
            slab_v = {}
            pe_v = {}
            ev = {}
            dve_v = {}
            xs_v = {}
            dma_v = {}
            w3 = wsrc.ap().rearrange("p (o co) -> p o co", co=4096)

            def load_slab(oc):
                wait_led("g", "WSL%d" % (oc % 2))
                if oc - 2 in slab_v:
                    wt("g", "GW%d" % (oc % 2), slab_v[oc - 2])
                wt("g", "GW%d" % (oc % 2), cnt["GW%d" % (oc % 2)])
                slab_v[oc] = dma("g", "GW%d" % (oc % 2),
                                 WSL[:, (oc % 2) * 4096:(oc % 2 + 1) * 4096], w3[:, oc, :])

            def load_xs(g2):
                oc, st = divmod(g2, 2)
                if g2 < 2:
                    wait_led("y", "XS")
                    for s_, v_ in xs_raw:
                        wt("y", s_, v_)
                else:
                    wt("y", "SD", dve_v[g2 - 2])
                    wt("y", "YX%d" % (g2 % 2), xs_v[g2 - 2])
                xs_v[g2] = dma("y", "YX%d" % (g2 % 2), XS[:, (g2 % 2) * 512:(g2 % 2 + 1) * 512],
                               resid_src[:, oc * 1024 + st * 512: oc * 1024 + (st + 1) * 512])

            load_slab(0)
            load_slab(1)
            load_xs(0)
            load_xs(1)
            for g2 in range(2 * n_oc):
                oc, st = divmod(g2, 2)
                bank = PA if g2 % 2 == 0 else PB
                wt("t", "GW%d" % (oc % 2), slab_v[oc])
                if g2 - 2 in ev:
                    wt("t", ev[g2 - 2][0], ev[g2 - 2][1])
                if g2 == 0:
                    for s_, v_ in pre_waits:
                        wt("t", s_, v_)
                    wait_led("t", "bank_PA")
                if g2 == 1:
                    wait_led("t", "bank_PB")
                for ich in range(32):
                    lh = WSL[:, (oc % 2) * 4096 + ich * 128:(oc % 2) * 4096 + (ich + 1) * 128]
                    rc = rhs_col(ich, st)
                    last = (ich == 31) and not ffn
                    run("t", lambda e, l=lh, c=rc, s0=(ich == 0), s1=(ich == 31 and not ffn),
                        b=bank: e.matmul(b[:, :], l, AB[:, c:c + 512], start=s0, stop=s1),
                        "SP" if last else None)
                if ffn:
                    hc = HW2 + st * 512
                    run("t", lambda e, c=hc, o=oc, b=bank: e.matmul(
                        b[:, :], BM2S[:, o * 128:(o + 1) * 128], SB2[:, c:c + 512],
                        start=False, stop=True), "SP")
                pe_v[g2] = cnt["SP"]
                if st == 1 and oc + 2 < n_oc:
                    wt("g", "SP", pe_v[g2])
                    load_slab(oc + 2)
                if ffn:
                    wt("s", "SP", pe_v[g2])
                    if g2 - 2 in dve_v:
                        wt("s", "SD", dve_v[g2 - 2])
                    else:
                        wait_led("s", "US")
                    uu = US[:, (g2 % 2) * 512:(g2 % 2 + 1) * 512]
                    run("s", lambda e, u=uu, b=bank: e.activation(
                        u, b[:, :], AF.Copy, scale=0.5), "SA")
                    ev[g2] = ("SA", cnt["SA"])
                    wt("v", "SA", ev[g2][1])
                    wt("v", "YX%d" % (g2 % 2), xs_v[g2])
                    if g2 - 2 in dma_v:
                        wt("v", "YW%d" % (g2 % 2), dma_v[g2 - 2])
                    else:
                        wait_led("v", "U2")
                    u2 = U2[:, (g2 % 2) * 512:(g2 % 2 + 1) * 512]
                    run("v", lambda e, a=u2, u=uu, o=oc,
                        x=XS[:, (g2 % 2) * 512:(g2 % 2 + 1) * 512]:
                        e.scalar_tensor_tensor(a, u, HB[:, o:o + 1], x, ALU.add, ALU.add),
                        "SD")
                    dve_v[g2] = cnt["SD"]
                else:
                    wt("v", "SP", pe_v[g2])
                    wt("v", "YX%d" % (g2 % 2), xs_v[g2])
                    if g2 - 2 in dma_v:
                        wt("v", "YW%d" % (g2 % 2), dma_v[g2 - 2])
                    else:
                        wait_led("v", "U2")
                    u2 = U2[:, (g2 % 2) * 512:(g2 % 2 + 1) * 512]
                    run("v", lambda e, a=u2, b=bank, x=XS[:, (g2 % 2) * 512:(g2 % 2 + 1) * 512]:
                        e.tensor_tensor(a, b[:, :], x, ALU.add), "SD")
                    dve_v[g2] = cnt["SD"]
                    ev[g2] = ("SD", dve_v[g2])
                if g2 + 2 < 2 * n_oc:
                    load_xs(g2 + 2)
                wt("y", "SD", dve_v[g2])
                u2 = U2[:, (g2 % 2) * 512:(g2 % 2 + 1) * 512]
                dma_v[g2] = dma("y", "YW%d" % (g2 % 2),
                                dest[:, oc * 1024 + st * 512: oc * 1024 + (st + 1) * 512], u2)
            last = 2 * n_oc - 1
            set_led("bank_PA", ev[last - 1][0], ev[last - 1][1])
            set_led("bank_PB", ev[last][0], ev[last][1])
            set_led("WSL0", "SP", pe_v[last])
            set_led("WSL1", "SP", pe_v[last])
            set_led("XS", "SD", dve_v[last])
            set_led("U2", "YW%d" % (last % 2), dma_v[last])
            if ffn:
                set_led("US", "SD", dve_v[last])
            return (("YW0", dma_v[last - 1]), ("YW1", dma_v[last]))

        def emit_ffn(resid_src, dest, xs_raw=()):
            hw2v = 0
            for st in range(NST):
                pref = {}
                va = emit_ln(resid_src, st * 512, st * 16384,
                             g_hook=lambda p=pref: emit_lora_loads(p))
                hw2v = emit_lora(st, st * 16384, va, pref)
            return emit_oloop(wffd, lambda ich, st: st * 16384 + ich * 512,
                              resid_src, dest, ffn=True,
                              pre_waits=(("SD", hw2v),), xs_raw=xs_raw)

        v_x1_done = emit_ffn(xd, x1d)

        v_vtm = 0
        v_qkv_mm = 0
        for s_, v_ in v_x1_done:
            wt("g", s_, v_)
        for st in range(NST if upto >= 2 else 0):
            va = emit_ln(x1d, st * 512, T1B)

            def t1_ch(ch):
                return AB[:, T1B + ch * 512: T1B + (ch + 1) * 512]

            slab_v = {}
            ev = {}
            w3 = wqkvd.ap().rearrange("p (o co) -> p o co", co=4096)
            SLABS = [WSL[:, 0:4096], WSL[:, 4096:8192], SB2[:, 0:4096]]
            oc_order = list(range(32, 48)) + list(range(32))

            cast_v = {}
            pe_gate = {}

            def qload(pos):
                b3 = pos % 3
                ch = "GW%d" % b3
                oc = oc_order[pos]
                if b3 < 2:
                    wait_led("g", "WSL%d" % b3)
                else:
                    wait_led("g", "xb")
                    wait_led("g", "xb2")
                wt("g", ch, cnt[ch])
                vh = dma("g", ch, SLABS[b3][:, 0:2048], w3[:, oc, 0:2048])
                if pos - 1 in cast_v:
                    wt("y", "SD", cast_v[pos - 1])
                    wt("s", "SD", cast_v[pos - 1])
                else:
                    wait_led("y", "XS")
                    wait_led("s", "US")
                wt("y", "YX0", cnt["YX0"])
                vq1 = dma("y", "YX0", XS[:, :], w3[:, oc, 2048:3072])
                wt("s", "AC0", cnt["AC0"])
                vq2 = dma("s", "AC0", US[:, :], w3[:, oc, 3072:4096])
                wt("v", "YX0", vq1)
                wt("v", "AC0", vq2)
                if pos >= 3:
                    wt("v", "SP", pe_gate[pos - 3])
                else:
                    if b3 < 2:
                        wait_led("v", "WSL%d" % b3)
                    else:
                        wait_led("v", "xb")
                        wait_led("v", "xb2")
                run("v", lambda e, d=SLABS[b3][:, 2048:3072]: e.tensor_copy(d, XS[:, :]))
                run("v", lambda e, d=SLABS[b3][:, 3072:4096]: e.tensor_copy(d, US[:, :]),
                    "SD")
                cast_v[pos] = cnt["SD"]
                slab_v[pos] = (vh, cast_v[pos])

            qload(0)
            qload(1)
            qload(2)
            wt("t", "SD", va)
            for pos in range(48):
                oc = oc_order[pos]
                b3 = pos % 3
                bank = PA if pos % 2 == 0 else PB
                wt("t", "GW%d" % b3, slab_v[pos][0])
                wt("t", "SD", slab_v[pos][1])
                if pos - 2 in ev:
                    wt("t", ev[pos - 2][0], ev[pos - 2][1])
                if pos == 0:
                    wait_led("t", "bank_PA")
                if pos == 1:
                    wait_led("t", "bank_PB")
                for ich in range(32):
                    lh = SLABS[b3][:, ich * 128:(ich + 1) * 128]
                    run("t", lambda e, l=lh, r=t1_ch(ich), s0=(ich == 0), s1=(ich == 31),
                        b=bank: e.matmul(b[:, :], l, r, start=s0, stop=s1),
                        "SP" if ich == 31 else None)
                v_pe = cnt["SP"]
                v_qkv_mm = v_pe
                pe_gate[pos] = v_pe
                if pos + 3 < 48:
                    wt("g", "SP", v_pe)
                    wt("y", "SP", v_pe)
                    wt("s", "SP", v_pe)
                    qload(pos + 3)
                wt("s", "SP", v_pe)
                if oc < 32:
                    dst = AB[:, oc * 1024 + st * 512: oc * 1024 + (st + 1) * 512]
                    run("s", lambda e, d=dst, b=bank: e.activation(d, b[:, :], AF.Copy), "SA")
                    ev[pos] = ("SA", cnt["SA"])
                elif oc < 40:
                    c = oc - 32
                    dst = AB[:, KVB + c * 1024 + st * 512: KVB + c * 1024 + (st + 1) * 512]
                    run("s", lambda e, d=dst, b=bank: e.activation(d, b[:, :], AF.Copy), "SA")
                    ev[pos] = ("SA", cnt["SA"])
                else:
                    h = oc - 40
                    wait_led("s", "VST")
                    run("s", lambda e, b=bank: e.activation(
                        SB2[:, VSTC:VSTC + 512], b[:, :], AF.Copy), "SA")
                    v_vst = cnt["SA"]
                    ev[pos] = ("SA", v_vst)
                    wt("t", "SA", v_vst)
                    wait_led("t", "bank_PT8")
                    for tl in range(4):
                        run("t", lambda e, i=tl: e.transpose(
                            PT8[:, i * 128:(i + 1) * 128],
                            SB2[:, VSTC + i * 128: VSTC + (i + 1) * 128], IDEN),
                            "SP" if tl == 3 else None)
                    v_tp = cnt["SP"]
                    set_led("VST", "SP", v_tp)
                    wt("v", "SP", v_tp)
                    base = VTB + st * 4 * 1024
                    dstv = AB[:, base: base + 4 * 1024].rearrange(
                        "p (t x) -> p t x", x=1024)[:, :, h * 128:(h + 1) * 128]
                    srcv = PT8[:, 0:512].rearrange("p (t x) -> p t x", x=128)
                    run("v", lambda e, d=dstv, s=srcv: e.tensor_copy(d, s), "SD")
                    v_vtm = cnt["SD"]
                    set_led("bank_PT8", "SD", v_vtm)
                if st == 1 and pos == 15 and upto >= 2:
                    wt("y", "SA", cnt["SA"])
                    wt("y", "SD", v_vtm)
                    for c in range(8):
                        dma("y", "YP", kvpay[c * 128:(c + 1) * 128, :],
                            AB[:, KVB + c * 1024: KVB + (c + 1) * 1024])
                    for tl in range(8):
                        dma("y", "YP", kvpay[1024 + tl * 128: 1024 + (tl + 1) * 128, :],
                            AB[:, VTB + tl * 1024: VTB + (tl + 1) * 1024])
                    v_pay = cnt["YP"]
                    wt("g", "YP", v_pay)
                    run("g", lambda e: e.collective_compute(
                        "AllGather", ALU.bypass,
                        replica_groups=[[0, 1], [2, 3], [4, 5], [6, 7]],
                        ins=[kvpay.ap().opt()], outs=[kvgath.ap().opt()]), "SCC")
            set_led("bank_PA", ev[46][0], ev[46][1])
            set_led("bank_PB", ev[47][0], ev[47][1])
            set_led("WSL0", "SP", cnt["SP"])
            set_led("WSL1", "SP", cnt["SP"])
            set_led("xb2", "SP", cnt["SP"])
            set_led("XS", "SD", cnt["SD"])
            set_led("US", "SD", cnt["SD"])
        v_qk_done = cnt["SA"]

        if upto < 2:
            wt("y", "YW0", cnt["YW0"])
            wt("y", "YW1", cnt["YW1"])
            return _finish(nc, stack, ops, block_emit)
        wt("y", "SCC", 1)
        wt("y", "SP", v_qkv_mm)
        for s2 in range(2):
            for hh in range(8):
                src = kvgath[s2 * 2048 + hh * 128: s2 * 2048 + (hh + 1) * 128, :].rearrange(
                    "d (t k) -> d t k", k=128)
                dst = AB[:, T1B + hh * 2048: T1B + (hh + 1) * 2048].rearrange(
                    "p (t x) -> p t x", x=256)[:, :, s2 * 128:(s2 + 1) * 128]
                dma("y", "YG", dst, src)
        for s2 in range(2):
            for tl in range(8):
                src = kvgath[s2 * 2048 + 1024 + tl * 128: s2 * 2048 + 1024 + (tl + 1) * 128, :]
                p = 2 * tl + s2
                dma("y", "YG", AB[:, KVB + p * 1024: KVB + (p + 1) * 1024], src)
        v_kv_in = cnt["YG"]

        if upto < 3:
            wt("y", "YG", cnt["YG"])
            return _finish(nc, stack, ops, block_emit)
        PG2 = PT8.bitcast(F32)
        exp_v = {}
        zav_v = {}
        bcz_v = {}
        bcze_v = {}
        qa_v = {}
        wait_led("v", "bank_PA")
        wait_led("v", "bank_PB")
        wait_led("v", "bank_PC")
        wait_led("v", "bank_PD")
        run("v", lambda e: e.memset(P01[:, :], 0.0))
        run("v", lambda e: e.memset(P23[:, :], 0.0), "SD")
        v_stinit = cnt["SD"]
        STREAMS = (
            dict(sid=0, q2=0, NB=4, ST=P01, pbase=0, zrow=M_ROW),
            dict(sid=1, q2=1, NB=8, ST=P23, pbase=2048, zrow=X_ROW),
        )
        first_st = True
        for g in range(32):
            hh = g // 4

            def a_stb(S, k):
                ST = S["ST"]
                qs = max(k - S["q2"] * 4, 0) * 128
                qbase = g * 1024 + S["q2"] * 512
                if (g, S["sid"], k - 1) in exp_v:
                    wt("t", "SA", exp_v[(g, S["sid"], k - 1)])
                elif k == 0 and (g - 1, S["sid"], S["NB"] - 1) in exp_v:
                    wt("t", "SA", exp_v[(g - 1, S["sid"], S["NB"] - 1)])
                for ti in range(2):
                    t = 2 * k + ti
                    kcol = T1B + hh * 2048 + t * 128
                    dst = ST[:, ti * 512 + qs: (ti + 1) * 512]
                    run("t", lambda e, kk=kcol, q=qbase + qs, w=512 - qs, d=dst:
                        e.matmul(d, AB[:, kk:kk + 128], AB[:, q:q + w],
                                 start=True, stop=True), "SP" if ti == 1 else None)
                return cnt["SP"]

            def a_exp(S, k, st_v):
                ST = S["ST"]
                qs = max(k - S["q2"] * 4, 0) * 128
                pb = S["pbase"] + (k % 2) * 1024
                wt("s", "SP", st_v)
                if (g, S["sid"], k - 2) in zav_v:
                    wt("s", "SP", zav_v[(g, S["sid"], k - 2)])
                elif k < 2 and (g - 1, S["sid"], S["NB"] - 1) in zav_v:
                    wt("s", "SP", zav_v[(g - 1, S["sid"], S["NB"] - 1)])
                run("s", lambda e, o=qs, p=pb: e.activation(
                    PT[:, p + o:p + 1024], ST[:, o:1024], AF.Exp,
                    scale=SCALE_ATTN), "SA")
                exp_v[(g, S["sid"], k)] = cnt["SA"]
                if k >= S["q2"] * 4:
                    wt("v", "SA", exp_v[(g, S["sid"], k)])
                    for ti in range(2):
                        t = 2 * k + ti
                        pcol = pb + ti * 512 + qs
                        run("v", lambda e, p=pcol, t_=t: e.tensor_tensor(
                            PT[:, p:p + 128], PT[:, p:p + 128],
                            MSK[:, t_ * 128:(t_ + 1) * 128], ALU.mult),
                            "SD" if ti == 1 else None)
                    return cnt["SD"]
                return None

            def a_zav(S, k, mv):
                qs = max(k - S["q2"] * 4, 0) * 128
                pb = S["pbase"] + (k % 2) * 1024
                av = PE_ if S["sid"] == 0 else PF
                zt = PG if S["sid"] == 0 else PG2
                T2 = 2 * S["NB"]
                if mv:
                    wt("t", "SD", mv)
                else:
                    wt("t", "SA", exp_v[(g, S["sid"], k)])
                if k == 0:
                    if (g - 1, S["sid"]) in bcze_v:
                        wt("t", "SA", bcze_v[(g - 1, S["sid"])])
                    elif S["sid"] == 1:
                        wait_led("t", "bank_PT8")
                    else:
                        wait_led("t", "bank_PG")
                    if (g - 1, S["sid"]) in qa_v:
                        wt("t", "SD", qa_v[(g - 1, S["sid"])])
                    else:
                        wait_led("t", "bank_PE_" if S["sid"] == 0 else "bank_PF")
                for ti in range(2):
                    t = 2 * k + ti
                    pcol = pb + ti * 512 + qs
                    pw = 512 - qs
                    s0 = (t == 0)
                    s1 = (t == T2 - 1)
                    run("t", lambda e, o=qs, p=pcol, w=pw, s0_=s0, s1_=s1, z=zt:
                        e.matmul(z[0:1, o:512], ONES, PT[:, p:p + w],
                                 start=s0_, stop=s1_))
                    vcol = KVB + t * 1024 + hh * 128
                    run("t", lambda e, v=vcol, o=qs, p=pcol, w=pw, s0_=s0, s1_=s1,
                        b=av: e.matmul(b[:, o:512], AB[:, v:v + 128], PT[:, p:p + w],
                                       start=s0_, stop=s1_),
                        "SP" if ti == 1 else None)
                return cnt["SP"]

            def a_tail(S):
                av = PE_ if S["sid"] == 0 else PF
                zt = PG if S["sid"] == 0 else PG2
                zr = S["zrow"]
                qbase = g * 1024 + S["q2"] * 512
                wt("v", "SP", zav_v[(g, S["sid"], S["NB"] - 1)])
                if (g - 1, S["sid"]) in bcz_v:
                    wt("v", "SP", bcz_v[(g - 1, S["sid"])])
                else:
                    wait_led("v", "rows")
                run("v", lambda e, z=zt, r=zr: e.reciprocal(r, z[0:1, 0:512]), "SD")
                v_zi = cnt["SD"]
                wt("t", "SD", v_zi)
                run("t", lambda e, z=zt, r=zr: e.matmul(
                    z[:, 0:512], ONEROW, r, start=True, stop=True), "SP")
                bcz_v[(g, S["sid"])] = cnt["SP"]
                wt("s", "SP", bcz_v[(g, S["sid"])])
                if (g - 1, S["sid"]) in qa_v:
                    wt("s", "SD", qa_v[(g - 1, S["sid"])])
                else:
                    wait_led("s", "US")
                uu = US[:, S["sid"] * 512:(S["sid"] + 1) * 512]
                run("s", lambda e, u=uu, z=zt: e.activation(u, z[:, 0:512], AF.Copy), "SA")
                bcze_v[(g, S["sid"])] = cnt["SA"]
                wt("v", "SA", bcze_v[(g, S["sid"])])
                run("v", lambda e, q=qbase, b=av, u=uu: e.tensor_tensor(
                    AB[:, q:q + 512], b[:, :], u, ALU.mult), "SD")
                qa_v[(g, S["sid"])] = cnt["SD"]

            if first_st:
                wt("t", "YG", v_kv_in)
                wt("t", "SD", v_stinit)
                first_st = False
            A, B = STREAMS
            mv = {}
            stv = {}
            for k in range(9):
                for S in (A, B):
                    if k < S["NB"]:
                        stv[(S["sid"], k)] = a_stb(S, k)
                for S in (A, B):
                    if 1 <= k <= S["NB"]:
                        zav_v[(g, S["sid"], k - 1)] = a_zav(S, k - 1, mv[(S["sid"], k - 1)])
                for S in (A, B):
                    if k < S["NB"]:
                        mv[(S["sid"], k)] = a_exp(S, k, stv[(S["sid"], k)])
            a_tail(A)
            a_tail(B)
        v_attn_done = qa_v[(31, 1)]
        set_led("bank_PA", "SA", exp_v[(31, 1, 7)])
        set_led("bank_PB", "SA", exp_v[(31, 1, 7)])
        set_led("bank_PC", "SA", exp_v[(31, 1, 7)])
        set_led("bank_PD", "SA", exp_v[(31, 1, 7)])
        set_led("bank_PE_", "SD", qa_v[(31, 0)])
        set_led("bank_PF", "SD", qa_v[(31, 1)])
        set_led("bank_PG", "SA", bcze_v[(31, 0)])
        set_led("bank_PT8", "SA", bcze_v[(31, 1)])
        set_led("rows", "SP", max(bcz_v[(31, 0)], bcz_v[(31, 1)]))
        set_led("US", "SD", qa_v[(31, 1)])

        if debug:
            wt("y", "SD", v_attn_done)
            dma("y", "YP", qadump.ap(), AB[:, 0:32768])

        if upto < 4:
            if debug:
                wt("y", "YP", cnt["YP"])
            wt("y", "YW0", cnt["YW0"])
            wt("y", "YW1", cnt["YW1"])
            return _finish(nc, stack, ops, block_emit)
        v_x2_done = emit_oloop(wdd, lambda ich, st: ich * 1024 + st * 512, x1d, x2d,
                               ffn=False, pre_waits=(("SD", v_attn_done),),
                               xs_raw=v_x1_done)

        if upto < 5:
            wt("y", "YW0", cnt["YW0"])
            wt("y", "YW1", cnt["YW1"])
            return _finish(nc, stack, ops, block_emit)
        for s_, v_ in v_x2_done:
            wt("g", s_, v_)
        if debug:
            wt("v", "YP", cnt["YP"])
        emit_ffn(x2d, outd, xs_raw=v_x2_done)
        wt("y", "YW0", cnt["YW0"])
        wt("y", "YW1", cnt["YW1"])
        if debug:
            wt("y", "YP", cnt["YP"])

        return _finish(nc, stack, ops, block_emit)


def block_emit(nc, ops):
    with nc.Block() as block:
        @block.tensor
        def _(e):
            for f in ops["t"]:
                f(e)

        @block.scalar
        def _(e):
            for f in ops["s"]:
                f(e)

        @block.vector
        def _(e):
            for f in ops["v"]:
                f(e)

        @block.gpsimd
        def _(e):
            for f in ops["g"]:
                f(e)

        @block.sync
        def _(e):
            for f in ops["y"]:
                f(e)


def _finish(nc, stack, ops, emit):
    try:
        emit(nc, ops)
    finally:
        stack.close()
    return nc


def _tile_lhs(w):
    O, I = w.shape
    noc, nch = O // 128, I // 128
    return np.ascontiguousarray(
        w.reshape(noc, 128, nch, 128).transpose(3, 0, 2, 1).reshape(128, noc * nch * 128))


def _tile_moving(wt):
    I, O = wt.shape
    return np.ascontiguousarray(
        wt.reshape(I // 128, 128, O).transpose(1, 0, 2).reshape(128, (I // 128) * O))


def _tile_x(tok_major):
    f = np.ascontiguousarray(tok_major.T)
    return np.ascontiguousarray(f.reshape(32, 128, 1024).transpose(1, 0, 2).reshape(128, 32768))


def _untile_x(arr):
    f = arr.reshape(128, 32, 1024).transpose(1, 0, 2).reshape(4096, 1024)
    return np.ascontiguousarray(f.T)


def _consts():
    cb = np.zeros((128, 512), np.float32)
    cb[:, 0] = 1.0
    for e in range(4):
        cb[e * 32:(e + 1) * 32, 1 + e] = 1.0
        cb[e * 32:(e + 1) * 32, 9 + 4 + e] = 1.0
    for e in range(8):
        cb[e, 17 + e * 16: 17 + (e + 1) * 16] = 1.0
    cb[:, 145:273] = np.eye(128, dtype=np.float32)
    cb[0, 273:401] = 1.0
    cf = np.zeros((128, 132), np.float32)
    cf[0, 0:128] = 1.0
    cf[0, 128] = 1e-5
    return cb.astype(BF), cf


def _mask(par):
    m = np.zeros((128, 2048), np.float32)
    own = OWN[par]
    for t in range(16):
        pb = PERM_ABS[t]
        ab = own[t // 2]
        if pb < ab:
            blk = np.ones((128, 128), np.float32)
        elif pb > ab:
            blk = np.zeros((128, 128), np.float32)
        else:
            blk = np.triu(np.ones((128, 128), np.float32))
        m[:, t * 128:(t + 1) * 128] = blk
    return m.astype(BF)


def kernel(L0_attn_q, L0_attn_w_qkv, L0_attn_w_dense, L0_ffn_W, L0_ffn_b,
           L0_ffn_Wq, L0_ffn_Wk, L0_ffn_A, L0_ffn_Bm, _debug=False):
    x = np.asarray(L0_attn_q, np.float32)
    wff_t = _tile_lhs(np.asarray(L0_ffn_W, np.float32))
    wqkv_t = _tile_lhs(np.asarray(L0_attn_w_qkv, np.float32))
    wd_t = _tile_lhs(np.asarray(L0_attn_w_dense, np.float32))
    wq_t = _tile_moving(np.ascontiguousarray(np.asarray(L0_ffn_Wq, np.float32).T))
    wk_t = _tile_moving(np.ascontiguousarray(np.asarray(L0_ffn_Wk, np.float32).T))
    a2 = np.asarray(L0_ffn_A, np.float32).reshape(128, 4096)
    at_t = _tile_moving(np.ascontiguousarray(a2.T))
    bm2_t = np.ascontiguousarray(np.asarray(L0_ffn_Bm, np.float32).transpose(0, 2, 1)
                                 .reshape(128, 4096))
    b2_t = np.ascontiguousarray(np.asarray(L0_ffn_b, np.float32).reshape(32, 128).T)
    cb, cf = _consts()
    masks = [_mask(0), _mask(1)]

    in_maps = []
    for c in range(8):
        seq, par = c // 2, c % 2
        rows = np.concatenate([x[seq, ab * 128:(ab + 1) * 128, :] for ab in OWN[par]], axis=0)
        in_maps.append({
            "x": _tile_x(rows), "wff": wff_t, "wqkv": wqkv_t, "wd": wd_t,
            "wq": wq_t, "wk": wk_t, "at": at_t, "bm2": bm2_t, "b2": b2_t,
            "cb": cb, "cf": cf, "mask": masks[par],
        })

    nc = build_nc(debug=_debug)
    res = run_bass_kernel_spmd(nc, in_maps, core_ids=list(range(8)))

    y = np.empty((4, 2048, 4096), np.float32)
    for c in range(8):
        seq, par = c // 2, c % 2
        toks = _untile_x(res.results[c]["out"])
        for li, ab in enumerate(OWN[par]):
            y[seq, ab * 128:(ab + 1) * 128, :] = toks[li * 128:(li + 1) * 128]
    if _debug:
        return y, res
    return y


# version 11
# speedup vs baseline: 1.0407x; 1.0407x over previous
import numpy as np
import ml_dtypes
import concourse.bass as bass
import concourse.mybir as mybir
from concourse.bass_utils import run_bass_kernel_spmd

F32 = mybir.dt.float32
BF16 = mybir.dt.bfloat16
AF = mybir.ActivationFunctionType
ALU = mybir.AluOpType
BF = ml_dtypes.bfloat16

D = 4096
NCH = 32
SC = 1024
NST = 2
SCALE_ATTN = float(1.0 / np.sqrt(128.0))
SCALE_LORA = float(1.0 / np.sqrt(32.0))
INV_D = float(1.0 / 4096.0)

OWN = [[0, 3, 4, 7, 8, 11, 12, 15], [1, 2, 5, 6, 9, 10, 13, 14]]
PERM_ABS = [0, 1, 3, 2, 4, 5, 7, 6, 8, 9, 11, 10, 12, 13, 15, 14]

T1B = 32768
KVB = 49152
VTB = 57344
XBB = 0
SQB = 4096
LQ0 = 6144; LQ1 = 6656; LK0 = 7168; LK1 = 7680
LP0 = 8192; LP1 = 8704; LP8 = 9216; LH = 9728; LHW = 10240
HW2 = 10752
VSTC = 11776
RM = 0; RX = 512; RT = 1024


def build_nc(debug=False, upto=9):
    nc = bass.Bass("TRN2", target_bir_lowering=False, debug=False)

    xd = nc.dram_tensor("x", [128, NCH * SC], F32, kind="ExternalInput")
    wffd = nc.dram_tensor("wff", [128, 32 * 4096], F32, kind="ExternalInput")
    wqkvd = nc.dram_tensor("wqkv", [128, 48 * 4096], F32, kind="ExternalInput")
    wdd = nc.dram_tensor("wd", [128, 32 * 4096], F32, kind="ExternalInput")
    wqd = nc.dram_tensor("wq", [128, 32 * 256], F32, kind="ExternalInput")
    wkd = nc.dram_tensor("wk", [128, 32 * 256], F32, kind="ExternalInput")
    atd = nc.dram_tensor("at", [128, 32 * 128], F32, kind="ExternalInput")
    bm2d = nc.dram_tensor("bm2", [128, 4096], F32, kind="ExternalInput")
    b2d = nc.dram_tensor("b2", [128, 32], F32, kind="ExternalInput")
    cbd = nc.dram_tensor("cb", [128, 512], BF16, kind="ExternalInput")
    cfd = nc.dram_tensor("cf", [128, 132], F32, kind="ExternalInput")
    maskd = nc.dram_tensor("mask", [128, 2048], BF16, kind="ExternalInput")

    okind = dict(kind="ExternalOutput") if debug else {}
    x1d = nc.dram_tensor("x1d", [128, NCH * SC], F32, **okind)
    x2d = nc.dram_tensor("x2d", [128, NCH * SC], F32, **okind)
    kvpay = nc.dram_tensor("kvpay", [2048, 1024], BF16)
    kvgath = nc.dram_tensor("kvgath", [4096, 1024], BF16)
    outd = nc.dram_tensor("out", [128, NCH * SC], F32, kind="ExternalOutput")
    if debug:
        qadump = nc.dram_tensor("qadump", [128, 32768], BF16, kind="ExternalOutput")

    import contextlib
    with contextlib.ExitStack() as stack:
        en = stack.enter_context
        AB = en(nc.sbuf_tensor("AB", [128, 65536], BF16))
        WSL = en(nc.sbuf_tensor("WSL", [128, 8192], BF16))
        BM2S = en(nc.sbuf_tensor("BM2S", [128, 4096], BF16))
        SB2 = en(nc.sbuf_tensor("SB2", [128, 12288], BF16))
        PT = en(nc.sbuf_tensor("PTS", [128, 4096], BF16))
        MSK = en(nc.sbuf_tensor("MSK", [128, 2048], BF16))
        CBS = en(nc.sbuf_tensor("CBS", [128, 512], BF16))
        CFS = en(nc.sbuf_tensor("CFS", [128, 132], F32))
        HB = en(nc.sbuf_tensor("HB", [128, 32], F32))
        XS = en(nc.sbuf_tensor("XS", [128, 1024], F32))
        US = en(nc.sbuf_tensor("US", [128, 1024], F32))
        U2 = en(nc.sbuf_tensor("U2", [128, 1024], F32))
        ROWS = en(nc.sbuf_tensor("ROWS", [1, 1536], F32))

        P01 = en(nc.psum_tensor("P01", [128, 1024], F32))
        P23 = en(nc.psum_tensor("P23", [128, 1024], F32))
        P45 = en(nc.psum_tensor("P45", [128, 1024], F32))
        PG = en(nc.psum_tensor("PG", [128, 512], F32))
        PT8 = en(nc.psum_tensor("PT8", [128, 1024], BF16))
        PA = P01[:, 0:512]
        PB = P01[:, 512:1024]
        PC = P23[:, 0:512]
        PD = P23[:, 512:1024]
        PE_ = P45[:, 0:512]
        PF = P45[:, 512:1024]

        SEM_NAMES = ("SP", "SA", "SD", "SCC", "GC", "GW0", "GW1", "GW2", "GL", "GL1", "GX0", "GX1",
                     "YC", "YX0", "YX1", "YW0", "YW1", "YP", "YG")
        SEM = {n: en(nc.semaphore(n)) for n in SEM_NAMES}
        ops = {k: [] for k in ("t", "s", "v", "g", "y")}
        cnt = {n: 0 for n in SEM_NAMES}

        def wt(eng, sname, val):
            if val > 0:
                ops[eng].append(lambda e, _s=SEM[sname], _v=val: e.wait_ge(_s, _v))

        def run(eng, builder, inc=None, n=1):
            if inc is None:
                ops[eng].append(builder)
            else:
                bump(inc, n)
                ops[eng].append(lambda e, _b=builder, _s=SEM[inc], _n=n: _b(e).then_inc(_s, _n))

        def bump(sname, n=1):
            cnt[sname] += n
            return cnt[sname]

        def dma(eng, sname, out_ap, in_ap):
            run(eng, lambda e, o=out_ap, i=in_ap: e.dma_start(out=o, in_=i), sname, 16)
            return cnt[sname]

        led = {}

        def set_led(key, sname, val):
            led[key] = (sname, val)

        def wait_led(eng, key):
            if key in led:
                s, v = led[key]
                wt(eng, s, v)

        def run_ser(eng, sem, builder):
            wt(eng, sem, cnt[sem])
            run(eng, builder, sem)

        ONES = CBS[:, 0:1]
        ONES8 = CBS[0:8, 0:1]
        SEL0 = CBS[:, 1:9]
        SEL1 = CBS[:, 9:17]
        EXP8 = CBS[0:8, 17:145]
        IDEN = CBS[:, 145:273]
        ONEROW = CFS[0:1, 0:128]
        EPS = CFS[0:1, 128:129]
        M_ROW = ROWS[0:1, RM:RM + 512]
        X_ROW = ROWS[0:1, RX:RX + 512]
        T_ROW = ROWS[0:1, RT:RT + 512]

        dma("g", "GC", CBS[:, :], cbd[:, :])
        dma("y", "YC", CFS[:, :], cfd[:, :])
        dma("g", "GC", MSK[:, :], maskd[:, :])
        dma("y", "YC", HB[:, :], b2d[:, :])
        dma("g", "GC", BM2S[:, :], bm2d[:, :])
        v_consts_g = cnt["GC"]
        v_consts_x = cnt["YC"]
        wt("v", "YC", v_consts_x)
        run("v", lambda e: e.tensor_scalar_mul(HB[:, :], HB[:, :], 0.5), "SD")
        v_hb = cnt["SD"]
        wt("t", "GC", v_consts_g)
        wt("s", "SD", v_hb)
        wt("s", "YC", v_consts_x)
        wt("v", "GC", v_consts_g)

        def emit_ln(src, s_off, tdest, g_hook=None, src_ready=None):
            src3 = src.ap().rearrange("p (c s) -> p c s", s=SC)
            pe_stats = {}
            for grp in range(8):
                buf = grp % 2
                if grp < 2:
                    wait_led("g", "xb")
                    wait_led("g", "xb2")
                else:
                    wt("g", "SP", pe_stats[grp - 2])
                if src_ready is not None:
                    for s_, v_ in src_ready(grp):
                        wt("g", s_, v_)
                xbv = SB2[:, XBB + buf * 2048: XBB + (buf + 1) * 2048].rearrange(
                    "p (c s) -> p c s", s=512)
                xch = "GX%d" % buf
                wt("g", xch, cnt[xch])
                vload = dma("g", xch, xbv, src3[:, grp * 4:(grp + 1) * 4, s_off:s_off + 512])
                wt("v", xch, vload)
                if grp - 1 in pe_stats:
                    wt("v", "SP", pe_stats[grp - 1])
                elif grp == 0:
                    wait_led("v", "sq")
                for i in range(4):
                    xb_i = SB2[:, XBB + buf * 2048 + i * 512: XBB + buf * 2048 + (i + 1) * 512]
                    sq_i = SB2[:, SQB + i * 512: SQB + (i + 1) * 512]
                    run("v", lambda e, a=sq_i, b=xb_i: e.tensor_tensor(a, b, b, ALU.mult),
                        "SD" if i == 3 else None)
                v_sq = cnt["SD"]
                wt("t", xch, vload)
                if grp == 0:
                    wait_led("t", "bank_PG")
                    wait_led("t", "bank_PC")
                for i in range(4):
                    xb_i = SB2[:, XBB + buf * 2048 + i * 512: XBB + buf * 2048 + (i + 1) * 512]
                    st0 = (grp == 0 and i == 0)
                    sp1 = (grp == 7 and i == 3)
                    run("t", lambda e, r=xb_i, s0=st0, s1=sp1: e.matmul(
                        PG[0:1, :], ONES, r, start=s0, stop=s1))
                wt("t", "SD", v_sq)
                for i in range(4):
                    sq_i = SB2[:, SQB + i * 512: SQB + (i + 1) * 512]
                    st0 = (grp == 0 and i == 0)
                    sp1 = (grp == 7 and i == 3)
                    run("t", lambda e, r=sq_i, s0=st0, s1=sp1: e.matmul(
                        PC[0:1, :], ONES, r, start=s0, stop=s1),
                        "SP" if i == 3 else None)
                pe_stats[grp] = cnt["SP"]
            wt("v", "SP", pe_stats[7])
            wait_led("v", "rows")
            run_ser("v", "SD", lambda e: e.tensor_scalar_mul(M_ROW, PG[0:1, :], INV_D))
            run_ser("v", "SD", lambda e: e.tensor_tensor(T_ROW, M_ROW, M_ROW, ALU.mult))
            run_ser("v", "SD", lambda e: e.scalar_tensor_tensor(
                X_ROW, PC[0:1, :], INV_D, T_ROW, ALU.mult, ALU.subtract))
            v_var = cnt["SD"]
            set_led("bank_PG", "SD", v_var)
            set_led("bank_PC", "SD", v_var)
            wt("s", "SD", v_var)
            run_ser("s", "SA", lambda e: e.activation(T_ROW, X_ROW, AF.Ln, bias=EPS, scale=1.0))
            run_ser("s", "SA", lambda e: e.activation(X_ROW, T_ROW, AF.Exp, scale=-0.5))
            v_rstd = cnt["SA"]
            wt("v", "SA", v_rstd)
            run("v", lambda e: e.scalar_tensor_tensor(
                T_ROW, M_ROW, -1.0, X_ROW, ALU.mult, ALU.mult), "SD")
            v_negmr = cnt["SD"]
            wt("t", "SD", v_negmr)
            wait_led("t", "bank_PE_")
            wait_led("t", "bank_PF")
            run("t", lambda e: e.matmul(PE_[:, :], ONEROW, X_ROW, start=True, stop=True))
            run("t", lambda e: e.matmul(PF[:, :], ONEROW, T_ROW, start=True, stop=True), "SP")
            v_bc = cnt["SP"]
            set_led("rows", "SP", v_bc)
            if g_hook is not None:
                g_hook()
            v_apply = {}
            for grp in range(8):
                buf = grp % 2
                if grp < 2:
                    wt("g", "SP", pe_stats[7])
                else:
                    wt("g", "SD", v_apply[grp - 2])
                xbv = SB2[:, XBB + buf * 2048: XBB + (buf + 1) * 2048].rearrange(
                    "p (c s) -> p c s", s=512)
                xch = "GX%d" % buf
                wt("g", xch, cnt[xch])
                vload = dma("g", xch, xbv, src3[:, grp * 4:(grp + 1) * 4, s_off:s_off + 512])
                wt("v", xch, vload)
                if grp == 0:
                    wt("v", "SP", v_bc)
                for i in range(4):
                    ch = grp * 4 + i
                    xb_i = SB2[:, XBB + buf * 2048 + i * 512: XBB + buf * 2048 + (i + 1) * 512]
                    tmp_i = SB2[:, SQB + i * 512: SQB + (i + 1) * 512]
                    tch = AB[:, tdest + ch * 512: tdest + (ch + 1) * 512]
                    run_ser("v", "SD", lambda e, a=tmp_i, b=xb_i:
                            e.tensor_tensor(a, b, PF[:, :], ALU.add))
                    run_ser("v", "SD", lambda e, a=tch, b=tmp_i:
                            e.tensor_tensor(a, b, PE_[:, :], ALU.mult))
                v_apply[grp] = cnt["SD"]
            set_led("bank_PE_", "SD", v_apply[7])
            set_led("bank_PF", "SD", v_apply[7])
            set_led("xb", "SD", v_apply[7])
            set_led("sq", "SD", v_apply[7])
            return v_apply[7]

        def emit_lora_loads(pref):
            wait_led("g", "WSL0")
            wait_led("g", "WSL1")
            wt("g", "GL", cnt["GL"])
            pref["wq0"] = dma("g", "GL", WSL[:, 0:4096], wqd[:, 0:4096])
            wt("g", "GL1", cnt["GL1"])
            pref["wq1"] = dma("g", "GL1", WSL[:, 4096:8192], wqd[:, 4096:8192])

        def emit_lora(st, tbase, v_apply, pref):
            def t_ch(ch):
                return AB[:, tbase + ch * 512: tbase + (ch + 1) * 512]

            wt("t", "GL", pref["wq0"])
            wt("t", "SD", v_apply)
            wait_led("t", "bank_PA")
            wait_led("t", "bank_PB")
            v_qh1 = 0
            for ich in range(32):
                if ich == 16:
                    wt("t", "GL1", pref["wq1"])
                for ch2, bank in ((0, PA), (1, PB)):
                    lh = WSL[:, ich * 256 + ch2 * 128: ich * 256 + (ch2 + 1) * 128]
                    run("t", lambda e, l=lh, r=t_ch(ich), s0=(ich == 0), s1=(ich == 31),
                        b=bank: e.matmul(b[:, :], l, r, start=s0, stop=s1),
                        "SP" if (ich in (15, 31) and ch2 == 1) else None)
                if ich == 15:
                    v_qh1 = cnt["SP"]
            v_qmm = cnt["SP"]
            wt("g", "SP", v_qh1)
            wt("g", "GL", cnt["GL"])
            wk0 = dma("g", "GL", WSL[:, 0:4096], wkd[:, 0:4096])
            wt("g", "SP", v_qmm)
            wt("g", "GL1", cnt["GL1"])
            wk1 = dma("g", "GL1", WSL[:, 4096:8192], wkd[:, 4096:8192])
            wt("s", "SP", v_qmm)
            run("s", lambda e: e.activation(SB2[:, LQ0:LQ0 + 512], PA, AF.Copy))
            run("s", lambda e: e.activation(SB2[:, LQ1:LQ1 + 512], PB, AF.Copy), "SA")
            v_qe = cnt["SA"]
            set_led("bank_PA", "SA", v_qe)
            set_led("bank_PB", "SA", v_qe)
            wt("t", "GL", wk0)
            wait_led("t", "bank_PC")
            wait_led("t", "bank_PD")
            v_kh1 = 0
            for ich in range(32):
                if ich == 16:
                    wt("t", "GL1", wk1)
                for ch2, bank in ((0, PC), (1, PD)):
                    lh = WSL[:, ich * 256 + ch2 * 128: ich * 256 + (ch2 + 1) * 128]
                    run("t", lambda e, l=lh, r=t_ch(ich), s0=(ich == 0), s1=(ich == 31),
                        b=bank: e.matmul(b[:, :], l, r, start=s0, stop=s1),
                        "SP" if (ich in (15, 31) and ch2 == 1) else None)
                if ich == 15:
                    v_kh1 = cnt["SP"]
            v_kmm = cnt["SP"]
            wt("g", "SP", v_kh1)
            wt("g", "GL", cnt["GL"])
            v_at = dma("g", "GL", WSL[:, 0:4096], atd[:, :])
            wt("s", "SP", v_kmm)
            run("s", lambda e: e.activation(SB2[:, LK0:LK0 + 512], PC, AF.Copy))
            run("s", lambda e: e.activation(SB2[:, LK1:LK1 + 512], PD, AF.Copy), "SA")
            v_ke = cnt["SA"]
            set_led("bank_PC", "SA", v_ke)
            set_led("bank_PD", "SA", v_ke)
            wt("v", "SA", v_ke)
            run("v", lambda e: e.tensor_tensor(SB2[:, LP0:LP0 + 512], SB2[:, LQ0:LQ0 + 512],
                                               SB2[:, LK0:LK0 + 512], ALU.mult))
            run("v", lambda e: e.tensor_tensor(SB2[:, LP1:LP1 + 512], SB2[:, LQ1:LQ1 + 512],
                                               SB2[:, LK1:LK1 + 512], ALU.mult), "SD")
            v_prod = cnt["SD"]
            wt("t", "SD", v_prod)
            wait_led("t", "bank_PG")
            run("t", lambda e: e.matmul(PG[0:8, :], SEL0, SB2[:, LP0:LP0 + 512],
                                        start=True, stop=False))
            run("t", lambda e: e.matmul(PG[0:8, :], SEL1, SB2[:, LP1:LP1 + 512],
                                        start=False, stop=True), "SP")
            v_s8 = cnt["SP"]
            wt("s", "SP", v_s8)
            run("s", lambda e: e.activation(SB2[0:8, LP8:LP8 + 512], PG[0:8, :], AF.Exp,
                                            scale=SCALE_LORA), "SA")
            v_p8 = cnt["SA"]
            wt("t", "SA", v_p8)
            run("t", lambda e: e.matmul(PG[64:65, :], ONES8, SB2[0:8, LP8:LP8 + 512],
                                        start=True, stop=True), "SP")
            v_z8 = cnt["SP"]
            wt("v", "SP", v_z8)
            wait_led("v", "rows")
            run("v", lambda e: e.reciprocal(M_ROW, PG[64:65, :]), "SD")
            v_zi = cnt["SD"]
            set_led("bank_PG", "SD", v_zi)
            wt("t", "SD", v_zi)
            wait_led("t", "bank_PE_")
            wait_led("t", "bank_PF")
            run("t", lambda e: e.matmul(PE_[:, :], EXP8, SB2[0:8, LP8:LP8 + 512],
                                        start=True, stop=True))
            run("t", lambda e: e.matmul(PF[:, :], ONEROW, M_ROW, start=True, stop=True), "SP")
            v_bc8 = cnt["SP"]
            set_led("rows", "SP", v_bc8)
            wt("t", "GL", v_at)
            for ich in range(32):
                lh = WSL[:, ich * 128:(ich + 1) * 128]
                run("t", lambda e, l=lh, r=t_ch(ich), s0=(ich == 0), s1=(ich == 31):
                    e.matmul(PA, l, r, start=s0, stop=s1),
                    "SP" if ich == 31 else None)
            v_h = cnt["SP"]
            set_led("WSL0", "SP", v_h)
            set_led("WSL1", "SP", v_kmm)
            wt("s", "SP", v_h)
            run("s", lambda e: e.activation(SB2[:, LH:LH + 512], PA, AF.Copy), "SA")
            v_he = cnt["SA"]
            set_led("bank_PA", "SA", v_he)
            wt("v", "SA", v_he)
            wt("v", "SP", v_bc8)
            run_ser("v", "SD", lambda e: e.tensor_tensor(SB2[:, LHW:LHW + 512], PE_[:, :],
                                                         SB2[:, LH:LH + 512], ALU.mult))
            hw2c = HW2 + st * 512
            run_ser("v", "SD", lambda e, c=hw2c: e.tensor_tensor(
                SB2[:, c:c + 512], PF[:, :], SB2[:, LHW:LHW + 512], ALU.mult))
            v_hw2 = cnt["SD"]
            set_led("bank_PE_", "SD", v_hw2)
            set_led("bank_PF", "SD", v_hw2)
            return v_hw2

        def emit_oloop(wsrc, rhs_col, resid_src, dest, ffn=True,
                       pre_waits=(), xs_raw=()):
            n_oc = 32
            slab_v = {}
            pe_v = {}
            ev = {}
            dve_v = {}
            xs_v = {}
            dma_v = {}
            w3 = wsrc.ap().rearrange("p (o co) -> p o co", co=4096)

            def load_slab(oc):
                wait_led("g", "WSL%d" % (oc % 2))
                if oc - 2 in slab_v:
                    wt("g", "GW%d" % (oc % 2), slab_v[oc - 2])
                wt("g", "GW%d" % (oc % 2), cnt["GW%d" % (oc % 2)])
                slab_v[oc] = dma("g", "GW%d" % (oc % 2),
                                 WSL[:, (oc % 2) * 4096:(oc % 2 + 1) * 4096], w3[:, oc, :])

            def load_xs(g2):
                oc, st = divmod(g2, 2)
                if g2 < 2:
                    wait_led("y", "XS")
                    for s_, v_ in xs_raw:
                        wt("y", s_, v_)
                else:
                    wt("y", "SD", dve_v[g2 - 2])
                    wt("y", "YX%d" % (g2 % 2), xs_v[g2 - 2])
                xs_v[g2] = dma("y", "YX%d" % (g2 % 2), XS[:, (g2 % 2) * 512:(g2 % 2 + 1) * 512],
                               resid_src[:, oc * 1024 + st * 512: oc * 1024 + (st + 1) * 512])

            load_slab(0)
            load_slab(1)
            load_xs(0)
            load_xs(1)
            for g2 in range(2 * n_oc):
                oc, st = divmod(g2, 2)
                bank = PA if g2 % 2 == 0 else PB
                wt("t", "GW%d" % (oc % 2), slab_v[oc])
                if g2 - 2 in ev:
                    wt("t", ev[g2 - 2][0], ev[g2 - 2][1])
                if g2 == 0:
                    for s_, v_ in pre_waits:
                        wt("t", s_, v_)
                    wait_led("t", "bank_PA")
                if g2 == 1:
                    wait_led("t", "bank_PB")
                for ich in range(32):
                    lh = WSL[:, (oc % 2) * 4096 + ich * 128:(oc % 2) * 4096 + (ich + 1) * 128]
                    rc = rhs_col(ich, st)
                    last = (ich == 31) and not ffn
                    run("t", lambda e, l=lh, c=rc, s0=(ich == 0), s1=(ich == 31 and not ffn),
                        b=bank: e.matmul(b[:, :], l, AB[:, c:c + 512], start=s0, stop=s1),
                        "SP" if last else None)
                if ffn:
                    hc = HW2 + st * 512
                    run("t", lambda e, c=hc, o=oc, b=bank: e.matmul(
                        b[:, :], BM2S[:, o * 128:(o + 1) * 128], SB2[:, c:c + 512],
                        start=False, stop=True), "SP")
                pe_v[g2] = cnt["SP"]
                if st == 1 and oc + 2 < n_oc:
                    wt("g", "SP", pe_v[g2])
                    load_slab(oc + 2)
                if ffn:
                    wt("s", "SP", pe_v[g2])
                    if g2 - 2 in dve_v:
                        wt("s", "SD", dve_v[g2 - 2])
                    else:
                        wait_led("s", "US")
                    uu = US[:, (g2 % 2) * 512:(g2 % 2 + 1) * 512]
                    run("s", lambda e, u=uu, b=bank: e.activation(
                        u, b[:, :], AF.Copy, scale=0.5), "SA")
                    ev[g2] = ("SA", cnt["SA"])
                    wt("v", "SA", ev[g2][1])
                    wt("v", "YX%d" % (g2 % 2), xs_v[g2])
                    if g2 - 2 in dma_v:
                        wt("v", "YW%d" % (g2 % 2), dma_v[g2 - 2])
                    else:
                        wait_led("v", "U2")
                    u2 = U2[:, (g2 % 2) * 512:(g2 % 2 + 1) * 512]
                    run("v", lambda e, a=u2, u=uu, o=oc,
                        x=XS[:, (g2 % 2) * 512:(g2 % 2 + 1) * 512]:
                        e.scalar_tensor_tensor(a, u, HB[:, o:o + 1], x, ALU.add, ALU.add),
                        "SD")
                    dve_v[g2] = cnt["SD"]
                else:
                    wt("v", "SP", pe_v[g2])
                    wt("v", "YX%d" % (g2 % 2), xs_v[g2])
                    if g2 - 2 in dma_v:
                        wt("v", "YW%d" % (g2 % 2), dma_v[g2 - 2])
                    else:
                        wait_led("v", "U2")
                    u2 = U2[:, (g2 % 2) * 512:(g2 % 2 + 1) * 512]
                    run("v", lambda e, a=u2, b=bank, x=XS[:, (g2 % 2) * 512:(g2 % 2 + 1) * 512]:
                        e.tensor_tensor(a, b[:, :], x, ALU.add), "SD")
                    dve_v[g2] = cnt["SD"]
                    ev[g2] = ("SD", dve_v[g2])
                if g2 + 2 < 2 * n_oc:
                    load_xs(g2 + 2)
                wt("y", "SD", dve_v[g2])
                u2 = U2[:, (g2 % 2) * 512:(g2 % 2 + 1) * 512]
                dma_v[g2] = dma("y", "YW%d" % (g2 % 2),
                                dest[:, oc * 1024 + st * 512: oc * 1024 + (st + 1) * 512], u2)
            last = 2 * n_oc - 1
            set_led("bank_PA", ev[last - 1][0], ev[last - 1][1])
            set_led("bank_PB", ev[last][0], ev[last][1])
            set_led("WSL0", "SP", pe_v[last])
            set_led("WSL1", "SP", pe_v[last])
            set_led("XS", "SD", dve_v[last])
            set_led("U2", "YW%d" % (last % 2), dma_v[last])
            if ffn:
                set_led("US", "SD", dve_v[last])
            return (("YW0", dma_v[last - 1]), ("YW1", dma_v[last])), dma_v

        def emit_ffn(resid_src, dest, xs_raw=(), src_ready=None):
            hw2v = 0
            for st in range(NST):
                pref = {}
                va = emit_ln(resid_src, st * 512, st * 16384,
                             g_hook=lambda p=pref: emit_lora_loads(p),
                             src_ready=src_ready if st == 0 else None)
                hw2v = emit_lora(st, st * 16384, va, pref)
            return emit_oloop(wffd, lambda ich, st: st * 16384 + ich * 512,
                              resid_src, dest, ffn=True,
                              pre_waits=(("SD", hw2v),), xs_raw=xs_raw)[0]

        v_x1_done = emit_ffn(xd, x1d)

        v_vtm = 0
        v_qkv_mm = 0
        for s_, v_ in v_x1_done:
            wt("g", s_, v_)
        for st in range(NST if upto >= 2 else 0):
            va = emit_ln(x1d, st * 512, T1B)

            def t1_ch(ch):
                return AB[:, T1B + ch * 512: T1B + (ch + 1) * 512]

            slab_v = {}
            ev = {}
            w3 = wqkvd.ap().rearrange("p (o co) -> p o co", co=4096)
            SLABS = [WSL[:, 0:4096], WSL[:, 4096:8192], SB2[:, 0:4096]]
            oc_order = list(range(32, 48)) + list(range(32))

            def qload(pos):
                b3 = pos % 3
                if b3 < 2:
                    wait_led("g", "WSL%d" % b3)
                else:
                    wait_led("g", "xb")
                    wait_led("g", "xb2")
                wt("g", "GW%d" % b3, cnt["GW%d" % b3])
                slab_v[pos] = dma("g", "GW%d" % b3, SLABS[b3], w3[:, oc_order[pos], :])

            qload(0)
            qload(1)
            qload(2)
            wt("t", "SD", va)
            for pos in range(48):
                oc = oc_order[pos]
                b3 = pos % 3
                bank = PA if pos % 2 == 0 else PB
                wt("t", "GW%d" % b3, slab_v[pos])
                if pos - 2 in ev:
                    wt("t", ev[pos - 2][0], ev[pos - 2][1])
                if pos == 0:
                    wait_led("t", "bank_PA")
                if pos == 1:
                    wait_led("t", "bank_PB")
                for ich in range(32):
                    lh = SLABS[b3][:, ich * 128:(ich + 1) * 128]
                    run("t", lambda e, l=lh, r=t1_ch(ich), s0=(ich == 0), s1=(ich == 31),
                        b=bank: e.matmul(b[:, :], l, r, start=s0, stop=s1),
                        "SP" if ich == 31 else None)
                v_pe = cnt["SP"]
                v_qkv_mm = v_pe
                if pos + 3 < 48:
                    wt("g", "SP", v_pe)
                    qload(pos + 3)
                wt("s", "SP", v_pe)
                if oc < 32:
                    dst = AB[:, oc * 1024 + st * 512: oc * 1024 + (st + 1) * 512]
                    run("s", lambda e, d=dst, b=bank: e.activation(d, b[:, :], AF.Copy), "SA")
                    ev[pos] = ("SA", cnt["SA"])
                elif oc < 40:
                    c = oc - 32
                    dst = AB[:, KVB + c * 1024 + st * 512: KVB + c * 1024 + (st + 1) * 512]
                    run("s", lambda e, d=dst, b=bank: e.activation(d, b[:, :], AF.Copy), "SA")
                    ev[pos] = ("SA", cnt["SA"])
                else:
                    h = oc - 40
                    wait_led("s", "VST")
                    run("s", lambda e, b=bank: e.activation(
                        SB2[:, VSTC:VSTC + 512], b[:, :], AF.Copy), "SA")
                    v_vst = cnt["SA"]
                    ev[pos] = ("SA", v_vst)
                    wt("t", "SA", v_vst)
                    wait_led("t", "bank_PT8")
                    for tl in range(4):
                        run("t", lambda e, i=tl: e.transpose(
                            PT8[:, i * 128:(i + 1) * 128],
                            SB2[:, VSTC + i * 128: VSTC + (i + 1) * 128], IDEN),
                            "SP" if tl == 3 else None)
                    v_tp = cnt["SP"]
                    set_led("VST", "SP", v_tp)
                    wt("v", "SP", v_tp)
                    base = VTB + st * 4 * 1024
                    dstv = AB[:, base: base + 4 * 1024].rearrange(
                        "p (t x) -> p t x", x=1024)[:, :, h * 128:(h + 1) * 128]
                    srcv = PT8[:, 0:512].rearrange("p (t x) -> p t x", x=128)
                    run("v", lambda e, d=dstv, s=srcv: e.tensor_copy(d, s), "SD")
                    v_vtm = cnt["SD"]
                    set_led("bank_PT8", "SD", v_vtm)
                if st == 1 and pos == 15 and upto >= 2:
                    wt("y", "SA", cnt["SA"])
                    wt("y", "SD", v_vtm)
                    for c in range(8):
                        dma("y", "YP", kvpay[c * 128:(c + 1) * 128, :],
                            AB[:, KVB + c * 1024: KVB + (c + 1) * 1024])
                    for tl in range(8):
                        dma("y", "YP", kvpay[1024 + tl * 128: 1024 + (tl + 1) * 128, :],
                            AB[:, VTB + tl * 1024: VTB + (tl + 1) * 1024])
                    v_pay = cnt["YP"]
                    wt("g", "YP", v_pay)
                    run("g", lambda e: e.collective_compute(
                        "AllGather", ALU.bypass,
                        replica_groups=[[0, 1], [2, 3], [4, 5], [6, 7]],
                        ins=[kvpay.ap().opt()], outs=[kvgath.ap().opt()]), "SCC")
            set_led("bank_PA", ev[46][0], ev[46][1])
            set_led("bank_PB", ev[47][0], ev[47][1])
            set_led("WSL0", "SP", cnt["SP"])
            set_led("WSL1", "SP", cnt["SP"])
            set_led("xb2", "SP", cnt["SP"])
        v_qk_done = cnt["SA"]

        if upto < 2:
            wt("y", "YW0", cnt["YW0"])
            wt("y", "YW1", cnt["YW1"])
            return _finish(nc, stack, ops, block_emit)
        wt("y", "SCC", 1)
        wt("y", "SP", v_qkv_mm)
        for s2 in range(2):
            for hh in range(8):
                src = kvgath[s2 * 2048 + hh * 128: s2 * 2048 + (hh + 1) * 128, :].rearrange(
                    "d (t k) -> d t k", k=128)
                dst = AB[:, T1B + hh * 2048: T1B + (hh + 1) * 2048].rearrange(
                    "p (t x) -> p t x", x=256)[:, :, s2 * 128:(s2 + 1) * 128]
                dma("y", "YG", dst, src)
        for s2 in range(2):
            for tl in range(8):
                src = kvgath[s2 * 2048 + 1024 + tl * 128: s2 * 2048 + 1024 + (tl + 1) * 128, :]
                p = 2 * tl + s2
                dma("y", "YG", AB[:, KVB + p * 1024: KVB + (p + 1) * 1024], src)
        v_kv_in = cnt["YG"]

        if upto < 3:
            wt("y", "YG", cnt["YG"])
            return _finish(nc, stack, ops, block_emit)
        PG2 = PT8.bitcast(F32)
        exp_v = {}
        zav_v = {}
        bcz_v = {}
        bcze_v = {}
        qa_v = {}
        wait_led("v", "bank_PA")
        wait_led("v", "bank_PB")
        wait_led("v", "bank_PC")
        wait_led("v", "bank_PD")
        run("v", lambda e: e.memset(P01[:, :], 0.0))
        run("v", lambda e: e.memset(P23[:, :], 0.0), "SD")
        v_stinit = cnt["SD"]
        STREAMS = (
            dict(sid=0, q2=0, NB=4, ST=P01, pbase=0, zrow=M_ROW),
            dict(sid=1, q2=1, NB=8, ST=P23, pbase=2048, zrow=X_ROW),
        )
        first_st = True
        for g in range(32):
            hh = g // 4

            def a_stb(S, k):
                ST = S["ST"]
                qs = max(k - S["q2"] * 4, 0) * 128
                qbase = g * 1024 + S["q2"] * 512
                if (g, S["sid"], k - 1) in exp_v:
                    wt("t", "SA", exp_v[(g, S["sid"], k - 1)])
                elif k == 0 and (g - 1, S["sid"], S["NB"] - 1) in exp_v:
                    wt("t", "SA", exp_v[(g - 1, S["sid"], S["NB"] - 1)])
                for ti in range(2):
                    t = 2 * k + ti
                    kcol = T1B + hh * 2048 + t * 128
                    dst = ST[:, ti * 512 + qs: (ti + 1) * 512]
                    run("t", lambda e, kk=kcol, q=qbase + qs, w=512 - qs, d=dst:
                        e.matmul(d, AB[:, kk:kk + 128], AB[:, q:q + w],
                                 start=True, stop=True), "SP" if ti == 1 else None)
                return cnt["SP"]

            def a_exp(S, k, st_v):
                ST = S["ST"]
                qs = max(k - S["q2"] * 4, 0) * 128
                pb = S["pbase"] + (k % 2) * 1024
                wt("s", "SP", st_v)
                if (g, S["sid"], k - 2) in zav_v:
                    wt("s", "SP", zav_v[(g, S["sid"], k - 2)])
                elif k < 2 and (g - 1, S["sid"], S["NB"] - 1) in zav_v:
                    wt("s", "SP", zav_v[(g - 1, S["sid"], S["NB"] - 1)])
                run("s", lambda e, o=qs, p=pb: e.activation(
                    PT[:, p + o:p + 1024], ST[:, o:1024], AF.Exp,
                    scale=SCALE_ATTN), "SA")
                exp_v[(g, S["sid"], k)] = cnt["SA"]
                if k >= S["q2"] * 4:
                    wt("v", "SA", exp_v[(g, S["sid"], k)])
                    for ti in range(2):
                        t = 2 * k + ti
                        pcol = pb + ti * 512 + qs
                        run("v", lambda e, p=pcol, t_=t: e.tensor_tensor(
                            PT[:, p:p + 128], PT[:, p:p + 128],
                            MSK[:, t_ * 128:(t_ + 1) * 128], ALU.mult),
                            "SD" if ti == 1 else None)
                    return cnt["SD"]
                return None

            def a_zav(S, k, mv):
                qs = max(k - S["q2"] * 4, 0) * 128
                pb = S["pbase"] + (k % 2) * 1024
                av = PE_ if S["sid"] == 0 else PF
                zt = PG if S["sid"] == 0 else PG2
                T2 = 2 * S["NB"]
                if mv:
                    wt("t", "SD", mv)
                else:
                    wt("t", "SA", exp_v[(g, S["sid"], k)])
                if k == 0:
                    if (g - 1, S["sid"]) in bcze_v:
                        wt("t", "SA", bcze_v[(g - 1, S["sid"])])
                    elif S["sid"] == 1:
                        wait_led("t", "bank_PT8")
                    else:
                        wait_led("t", "bank_PG")
                    if (g - 1, S["sid"]) in qa_v:
                        wt("t", "SD", qa_v[(g - 1, S["sid"])])
                    else:
                        wait_led("t", "bank_PE_" if S["sid"] == 0 else "bank_PF")
                for ti in range(2):
                    t = 2 * k + ti
                    pcol = pb + ti * 512 + qs
                    pw = 512 - qs
                    s0 = (t == 0)
                    s1 = (t == T2 - 1)
                    run("t", lambda e, o=qs, p=pcol, w=pw, s0_=s0, s1_=s1, z=zt:
                        e.matmul(z[0:1, o:512], ONES, PT[:, p:p + w],
                                 start=s0_, stop=s1_))
                    vcol = KVB + t * 1024 + hh * 128
                    run("t", lambda e, v=vcol, o=qs, p=pcol, w=pw, s0_=s0, s1_=s1,
                        b=av: e.matmul(b[:, o:512], AB[:, v:v + 128], PT[:, p:p + w],
                                       start=s0_, stop=s1_),
                        "SP" if ti == 1 else None)
                return cnt["SP"]

            def a_tail(S):
                av = PE_ if S["sid"] == 0 else PF
                zt = PG if S["sid"] == 0 else PG2
                zr = S["zrow"]
                qbase = g * 1024 + S["q2"] * 512
                wt("v", "SP", zav_v[(g, S["sid"], S["NB"] - 1)])
                if (g - 1, S["sid"]) in bcz_v:
                    wt("v", "SP", bcz_v[(g - 1, S["sid"])])
                else:
                    wait_led("v", "rows")
                run("v", lambda e, z=zt, r=zr: e.reciprocal(r, z[0:1, 0:512]), "SD")
                v_zi = cnt["SD"]
                wt("t", "SD", v_zi)
                run("t", lambda e, z=zt, r=zr: e.matmul(
                    z[:, 0:512], ONEROW, r, start=True, stop=True), "SP")
                bcz_v[(g, S["sid"])] = cnt["SP"]
                wt("s", "SP", bcz_v[(g, S["sid"])])
                if (g - 1, S["sid"]) in qa_v:
                    wt("s", "SD", qa_v[(g - 1, S["sid"])])
                else:
                    wait_led("s", "US")
                uu = US[:, S["sid"] * 512:(S["sid"] + 1) * 512]
                run("s", lambda e, u=uu, z=zt: e.activation(u, z[:, 0:512], AF.Copy), "SA")
                bcze_v[(g, S["sid"])] = cnt["SA"]
                wt("v", "SA", bcze_v[(g, S["sid"])])
                run("v", lambda e, q=qbase, b=av, u=uu: e.tensor_tensor(
                    AB[:, q:q + 512], b[:, :], u, ALU.mult), "SD")
                qa_v[(g, S["sid"])] = cnt["SD"]

            if first_st:
                wt("t", "YG", v_kv_in)
                wt("t", "SD", v_stinit)
                first_st = False
            A, B = STREAMS
            mv = {}
            stv = {}
            for k in range(9):
                for S in (A, B):
                    if k < S["NB"]:
                        stv[(S["sid"], k)] = a_stb(S, k)
                for S in (A, B):
                    if 1 <= k <= S["NB"]:
                        zav_v[(g, S["sid"], k - 1)] = a_zav(S, k - 1, mv[(S["sid"], k - 1)])
                for S in (A, B):
                    if k < S["NB"]:
                        mv[(S["sid"], k)] = a_exp(S, k, stv[(S["sid"], k)])
            a_tail(A)
            a_tail(B)
        v_attn_done = qa_v[(31, 1)]
        set_led("bank_PA", "SA", exp_v[(31, 1, 7)])
        set_led("bank_PB", "SA", exp_v[(31, 1, 7)])
        set_led("bank_PC", "SA", exp_v[(31, 1, 7)])
        set_led("bank_PD", "SA", exp_v[(31, 1, 7)])
        set_led("bank_PE_", "SD", qa_v[(31, 0)])
        set_led("bank_PF", "SD", qa_v[(31, 1)])
        set_led("bank_PG", "SA", bcze_v[(31, 0)])
        set_led("bank_PT8", "SA", bcze_v[(31, 1)])
        set_led("rows", "SP", max(bcz_v[(31, 0)], bcz_v[(31, 1)]))
        set_led("US", "SD", qa_v[(31, 1)])

        if debug:
            wt("y", "SD", v_attn_done)
            dma("y", "YP", qadump.ap(), AB[:, 0:32768])

        if upto < 4:
            if debug:
                wt("y", "YP", cnt["YP"])
            wt("y", "YW0", cnt["YW0"])
            wt("y", "YW1", cnt["YW1"])
            return _finish(nc, stack, ops, block_emit)
        v_x2_done, wr_x2 = emit_oloop(wdd, lambda ich, st: ich * 1024 + st * 512, x1d, x2d,
                                      ffn=False, pre_waits=(("SD", v_attn_done),),
                                      xs_raw=v_x1_done)

        def x2_ready(grp):
            oc = grp * 4 + 3
            return (("YW0", wr_x2[2 * oc]), ("YW1", wr_x2[2 * oc + 1]))

        if upto < 5:
            wt("y", "YW0", cnt["YW0"])
            wt("y", "YW1", cnt["YW1"])
            return _finish(nc, stack, ops, block_emit)
        if debug:
            wt("v", "YP", cnt["YP"])
        emit_ffn(x2d, outd, xs_raw=v_x2_done, src_ready=x2_ready)
        wt("y", "YW0", cnt["YW0"])
        wt("y", "YW1", cnt["YW1"])
        if debug:
            wt("y", "YP", cnt["YP"])

        return _finish(nc, stack, ops, block_emit)


def block_emit(nc, ops):
    with nc.Block() as block:
        @block.tensor
        def _(e):
            for f in ops["t"]:
                f(e)

        @block.scalar
        def _(e):
            for f in ops["s"]:
                f(e)

        @block.vector
        def _(e):
            for f in ops["v"]:
                f(e)

        @block.gpsimd
        def _(e):
            for f in ops["g"]:
                f(e)

        @block.sync
        def _(e):
            for f in ops["y"]:
                f(e)


def _finish(nc, stack, ops, emit):
    try:
        emit(nc, ops)
    finally:
        stack.close()
    return nc


def _tile_lhs(w):
    O, I = w.shape
    noc, nch = O // 128, I // 128
    return np.ascontiguousarray(
        w.reshape(noc, 128, nch, 128).transpose(3, 0, 2, 1).reshape(128, noc * nch * 128))


def _tile_moving(wt):
    I, O = wt.shape
    return np.ascontiguousarray(
        wt.reshape(I // 128, 128, O).transpose(1, 0, 2).reshape(128, (I // 128) * O))


def _tile_x(tok_major):
    f = np.ascontiguousarray(tok_major.T)
    return np.ascontiguousarray(f.reshape(32, 128, 1024).transpose(1, 0, 2).reshape(128, 32768))


def _untile_x(arr):
    f = arr.reshape(128, 32, 1024).transpose(1, 0, 2).reshape(4096, 1024)
    return np.ascontiguousarray(f.T)


def _consts():
    cb = np.zeros((128, 512), np.float32)
    cb[:, 0] = 1.0
    for e in range(4):
        cb[e * 32:(e + 1) * 32, 1 + e] = 1.0
        cb[e * 32:(e + 1) * 32, 9 + 4 + e] = 1.0
    for e in range(8):
        cb[e, 17 + e * 16: 17 + (e + 1) * 16] = 1.0
    cb[:, 145:273] = np.eye(128, dtype=np.float32)
    cf = np.zeros((128, 132), np.float32)
    cf[0, 0:128] = 1.0
    cf[0, 128] = 1e-5
    return cb.astype(BF), cf


def _mask(par):
    m = np.zeros((128, 2048), np.float32)
    own = OWN[par]
    for t in range(16):
        pb = PERM_ABS[t]
        ab = own[t // 2]
        if pb < ab:
            blk = np.ones((128, 128), np.float32)
        elif pb > ab:
            blk = np.zeros((128, 128), np.float32)
        else:
            blk = np.triu(np.ones((128, 128), np.float32))
        m[:, t * 128:(t + 1) * 128] = blk
    return m.astype(BF)


def kernel(L0_attn_q, L0_attn_w_qkv, L0_attn_w_dense, L0_ffn_W, L0_ffn_b,
           L0_ffn_Wq, L0_ffn_Wk, L0_ffn_A, L0_ffn_Bm, _debug=False):
    x = np.asarray(L0_attn_q, np.float32)
    wff_t = _tile_lhs(np.asarray(L0_ffn_W, np.float32))
    wqkv_t = _tile_lhs(np.asarray(L0_attn_w_qkv, np.float32))
    wd_t = _tile_lhs(np.asarray(L0_attn_w_dense, np.float32))
    wq_t = _tile_moving(np.ascontiguousarray(np.asarray(L0_ffn_Wq, np.float32).T))
    wk_t = _tile_moving(np.ascontiguousarray(np.asarray(L0_ffn_Wk, np.float32).T))
    a2 = np.asarray(L0_ffn_A, np.float32).reshape(128, 4096)
    at_t = _tile_moving(np.ascontiguousarray(a2.T))
    bm2_t = np.ascontiguousarray(np.asarray(L0_ffn_Bm, np.float32).transpose(0, 2, 1)
                                 .reshape(128, 4096))
    b2_t = np.ascontiguousarray(np.asarray(L0_ffn_b, np.float32).reshape(32, 128).T)
    cb, cf = _consts()
    masks = [_mask(0), _mask(1)]

    in_maps = []
    for c in range(8):
        seq, par = c // 2, c % 2
        rows = np.concatenate([x[seq, ab * 128:(ab + 1) * 128, :] for ab in OWN[par]], axis=0)
        in_maps.append({
            "x": _tile_x(rows), "wff": wff_t, "wqkv": wqkv_t, "wd": wd_t,
            "wq": wq_t, "wk": wk_t, "at": at_t, "bm2": bm2_t, "b2": b2_t,
            "cb": cb, "cf": cf, "mask": masks[par],
        })

    nc = build_nc(debug=_debug)
    res = run_bass_kernel_spmd(nc, in_maps, core_ids=list(range(8)))

    y = np.empty((4, 2048, 4096), np.float32)
    for c in range(8):
        seq, par = c // 2, c % 2
        toks = _untile_x(res.results[c]["out"])
        for li, ab in enumerate(OWN[par]):
            y[seq, ab * 128:(ab + 1) * 128, :] = toks[li * 128:(li + 1) * 128]
    if _debug:
        return y, res
    return y


# version 12
# speedup vs baseline: 1.0443x; 1.0035x over previous
import numpy as np
import ml_dtypes
import concourse.bass as bass
import concourse.mybir as mybir
from concourse.bass_utils import run_bass_kernel_spmd

F32 = mybir.dt.float32
BF16 = mybir.dt.bfloat16
AF = mybir.ActivationFunctionType
ALU = mybir.AluOpType
BF = ml_dtypes.bfloat16

D = 4096
NCH = 32
SC = 1024
NST = 2
SCALE_ATTN = float(1.0 / np.sqrt(128.0))
SCALE_LORA = float(1.0 / np.sqrt(32.0))
INV_D = float(1.0 / 4096.0)

OWN = [[0, 3, 4, 7, 8, 11, 12, 15], [1, 2, 5, 6, 9, 10, 13, 14]]
PERM_ABS = [0, 1, 3, 2, 4, 5, 7, 6, 8, 9, 11, 10, 12, 13, 15, 14]

T1B = 32768
KVB = 49152
VTB = 57344
XBB = 0
SQB = 4096
LQ0 = 6144; LQ1 = 6656; LK0 = 7168; LK1 = 7680
LP0 = 8192; LP1 = 8704; LP8 = 9216; LH = 9728; LHW = 10240
HW2 = 10752
VSTC = 11776
RM = 0; RX = 512; RT = 1024


def build_nc(debug=False, upto=9):
    nc = bass.Bass("TRN2", target_bir_lowering=False, debug=False)

    xd = nc.dram_tensor("x", [128, NCH * SC], F32, kind="ExternalInput")
    wffd = nc.dram_tensor("wff", [128, 32 * 4096], F32, kind="ExternalInput")
    wqkvd = nc.dram_tensor("wqkv", [128, 48 * 4096], F32, kind="ExternalInput")
    wdd = nc.dram_tensor("wd", [128, 32 * 4096], F32, kind="ExternalInput")
    wqd = nc.dram_tensor("wq", [128, 32 * 256], F32, kind="ExternalInput")
    wkd = nc.dram_tensor("wk", [128, 32 * 256], F32, kind="ExternalInput")
    atd = nc.dram_tensor("at", [128, 32 * 128], F32, kind="ExternalInput")
    bm2d = nc.dram_tensor("bm2", [128, 4096], F32, kind="ExternalInput")
    b2d = nc.dram_tensor("b2", [128, 32], F32, kind="ExternalInput")
    cbd = nc.dram_tensor("cb", [128, 512], BF16, kind="ExternalInput")
    cfd = nc.dram_tensor("cf", [128, 132], F32, kind="ExternalInput")
    maskd = nc.dram_tensor("mask", [128, 2048], BF16, kind="ExternalInput")

    okind = dict(kind="ExternalOutput") if debug else {}
    x1d = nc.dram_tensor("x1d", [128, NCH * SC], F32, **okind)
    x2d = nc.dram_tensor("x2d", [128, NCH * SC], F32, **okind)
    kvpay = nc.dram_tensor("kvpay", [2048, 1024], BF16)
    kvgath = nc.dram_tensor("kvgath", [4096, 1024], BF16)
    outd = nc.dram_tensor("out", [128, NCH * SC], F32, kind="ExternalOutput")
    if debug:
        qadump = nc.dram_tensor("qadump", [128, 32768], BF16, kind="ExternalOutput")

    import contextlib
    with contextlib.ExitStack() as stack:
        en = stack.enter_context
        AB = en(nc.sbuf_tensor("AB", [128, 65536], BF16))
        WSL = en(nc.sbuf_tensor("WSL", [128, 8192], BF16))
        BM2S = en(nc.sbuf_tensor("BM2S", [128, 4096], BF16))
        SB2 = en(nc.sbuf_tensor("SB2", [128, 12288], BF16))
        PT = en(nc.sbuf_tensor("PTS", [128, 4096], BF16))
        MSK = en(nc.sbuf_tensor("MSK", [128, 2048], BF16))
        CBS = en(nc.sbuf_tensor("CBS", [128, 512], BF16))
        CFS = en(nc.sbuf_tensor("CFS", [128, 132], F32))
        HB = en(nc.sbuf_tensor("HB", [128, 32], F32))
        XS = en(nc.sbuf_tensor("XS", [128, 1024], F32))
        US = en(nc.sbuf_tensor("US", [128, 1024], F32))
        U2 = en(nc.sbuf_tensor("U2", [128, 1024], F32))
        ROWS = en(nc.sbuf_tensor("ROWS", [1, 1536], F32))

        P01 = en(nc.psum_tensor("P01", [128, 1024], F32))
        P23 = en(nc.psum_tensor("P23", [128, 1024], F32))
        P45 = en(nc.psum_tensor("P45", [128, 1024], F32))
        PG = en(nc.psum_tensor("PG", [128, 512], F32))
        PT8 = en(nc.psum_tensor("PT8", [128, 1024], BF16))
        PA = P01[:, 0:512]
        PB = P01[:, 512:1024]
        PC = P23[:, 0:512]
        PD = P23[:, 512:1024]
        PE_ = P45[:, 0:512]
        PF = P45[:, 512:1024]

        SEM_NAMES = ("SP", "SA", "SD", "SCC", "GC", "GW0", "GW1", "GW2", "GL", "GL1", "GX0", "GX1",
                     "YC", "YX0", "YX1", "YW0", "YW1", "YP", "YG")
        SEM = {n: en(nc.semaphore(n)) for n in SEM_NAMES}
        ops = {k: [] for k in ("t", "s", "v", "g", "y")}
        cnt = {n: 0 for n in SEM_NAMES}

        def wt(eng, sname, val):
            if val > 0:
                ops[eng].append(lambda e, _s=SEM[sname], _v=val: e.wait_ge(_s, _v))

        def run(eng, builder, inc=None, n=1):
            if inc is None:
                ops[eng].append(builder)
            else:
                bump(inc, n)
                ops[eng].append(lambda e, _b=builder, _s=SEM[inc], _n=n: _b(e).then_inc(_s, _n))

        def bump(sname, n=1):
            cnt[sname] += n
            return cnt[sname]

        def dma(eng, sname, out_ap, in_ap):
            run(eng, lambda e, o=out_ap, i=in_ap: e.dma_start(out=o, in_=i), sname, 16)
            return cnt[sname]

        led = {}

        def set_led(key, sname, val):
            led[key] = (sname, val)

        def wait_led(eng, key):
            if key in led:
                s, v = led[key]
                wt(eng, s, v)

        def run_ser(eng, sem, builder):
            wt(eng, sem, cnt[sem])
            run(eng, builder, sem)

        ONES = CBS[:, 0:1]
        ONES8 = CBS[0:8, 0:1]
        SEL0 = CBS[:, 1:9]
        SEL1 = CBS[:, 9:17]
        EXP8 = CBS[0:8, 17:145]
        IDEN = CBS[:, 145:273]
        ONEROW = CFS[0:1, 0:128]
        ONEROWB = CBS[0:1, 273:401]
        EPS = CFS[0:1, 128:129]
        M_ROW = ROWS[0:1, RM:RM + 512]
        X_ROW = ROWS[0:1, RX:RX + 512]
        T_ROW = ROWS[0:1, RT:RT + 512]

        dma("g", "GC", CBS[:, :], cbd[:, :])
        dma("y", "YC", CFS[:, :], cfd[:, :])
        dma("g", "GC", MSK[:, :], maskd[:, :])
        dma("y", "YC", HB[:, :], b2d[:, :])
        dma("g", "GC", BM2S[:, :], bm2d[:, :])
        v_consts_g = cnt["GC"]
        v_consts_x = cnt["YC"]
        wt("v", "YC", v_consts_x)
        run("v", lambda e: e.tensor_scalar_mul(HB[:, :], HB[:, :], 0.5), "SD")
        v_hb = cnt["SD"]
        wt("t", "GC", v_consts_g)
        wt("s", "SD", v_hb)
        wt("s", "YC", v_consts_x)
        wt("v", "GC", v_consts_g)

        def emit_ln(src, s_off, tdest, g_hook=None, src_ready=None):
            src3 = src.ap().rearrange("p (c s) -> p c s", s=SC)
            pe_stats = {}
            for grp in range(8):
                buf = grp % 2
                if grp < 2:
                    wait_led("g", "xb")
                    wait_led("g", "xb2")
                else:
                    wt("g", "SP", pe_stats[grp - 2])
                if src_ready is not None:
                    for s_, v_ in src_ready(grp):
                        wt("g", s_, v_)
                xbv = SB2[:, XBB + buf * 2048: XBB + (buf + 1) * 2048].rearrange(
                    "p (c s) -> p c s", s=512)
                xch = "GX%d" % buf
                wt("g", xch, cnt[xch])
                vload = dma("g", xch, xbv, src3[:, grp * 4:(grp + 1) * 4, s_off:s_off + 512])
                wt("v", xch, vload)
                if grp - 1 in pe_stats:
                    wt("v", "SP", pe_stats[grp - 1])
                elif grp == 0:
                    wait_led("v", "sq")
                for i in range(4):
                    xb_i = SB2[:, XBB + buf * 2048 + i * 512: XBB + buf * 2048 + (i + 1) * 512]
                    sq_i = SB2[:, SQB + i * 512: SQB + (i + 1) * 512]
                    run("v", lambda e, a=sq_i, b=xb_i: e.tensor_tensor(a, b, b, ALU.mult),
                        "SD" if i == 3 else None)
                v_sq = cnt["SD"]
                wt("t", xch, vload)
                if grp == 0:
                    wait_led("t", "bank_PG")
                    wait_led("t", "bank_PC")
                for i in range(4):
                    xb_i = SB2[:, XBB + buf * 2048 + i * 512: XBB + buf * 2048 + (i + 1) * 512]
                    st0 = (grp == 0 and i == 0)
                    sp1 = (grp == 7 and i == 3)
                    run("t", lambda e, r=xb_i, s0=st0, s1=sp1: e.matmul(
                        PG[0:1, :], ONES, r, start=s0, stop=s1))
                wt("t", "SD", v_sq)
                for i in range(4):
                    sq_i = SB2[:, SQB + i * 512: SQB + (i + 1) * 512]
                    st0 = (grp == 0 and i == 0)
                    sp1 = (grp == 7 and i == 3)
                    run("t", lambda e, r=sq_i, s0=st0, s1=sp1: e.matmul(
                        PC[0:1, :], ONES, r, start=s0, stop=s1),
                        "SP" if i == 3 else None)
                pe_stats[grp] = cnt["SP"]
            wt("v", "SP", pe_stats[7])
            wait_led("v", "rows")
            run_ser("v", "SD", lambda e: e.tensor_scalar_mul(M_ROW, PG[0:1, :], INV_D))
            run_ser("v", "SD", lambda e: e.tensor_tensor(T_ROW, M_ROW, M_ROW, ALU.mult))
            run_ser("v", "SD", lambda e: e.scalar_tensor_tensor(
                X_ROW, PC[0:1, :], INV_D, T_ROW, ALU.mult, ALU.subtract))
            v_var = cnt["SD"]
            set_led("bank_PG", "SD", v_var)
            set_led("bank_PC", "SD", v_var)
            wt("s", "SD", v_var)
            run_ser("s", "SA", lambda e: e.activation(T_ROW, X_ROW, AF.Ln, bias=EPS, scale=1.0))
            run_ser("s", "SA", lambda e: e.activation(X_ROW, T_ROW, AF.Exp, scale=-0.5))
            v_rstd = cnt["SA"]
            wt("v", "SA", v_rstd)
            run("v", lambda e: e.scalar_tensor_tensor(
                T_ROW, M_ROW, -1.0, X_ROW, ALU.mult, ALU.mult), "SD")
            v_negmr = cnt["SD"]
            wt("t", "SD", v_negmr)
            wait_led("t", "bank_PE_")
            wait_led("t", "bank_PF")
            run("t", lambda e: e.matmul(PE_[:, :], ONEROW, X_ROW, start=True, stop=True))
            run("t", lambda e: e.matmul(PF[:, :], ONEROW, T_ROW, start=True, stop=True), "SP")
            v_bc = cnt["SP"]
            set_led("rows", "SP", v_bc)
            if g_hook is not None:
                g_hook()
            v_apply = {}
            for grp in range(8):
                buf = grp % 2
                if grp < 2:
                    wt("g", "SP", pe_stats[7])
                else:
                    wt("g", "SD", v_apply[grp - 2])
                xbv = SB2[:, XBB + buf * 2048: XBB + (buf + 1) * 2048].rearrange(
                    "p (c s) -> p c s", s=512)
                xch = "GX%d" % buf
                wt("g", xch, cnt[xch])
                vload = dma("g", xch, xbv, src3[:, grp * 4:(grp + 1) * 4, s_off:s_off + 512])
                wt("v", xch, vload)
                if grp == 0:
                    wt("v", "SP", v_bc)
                for i in range(4):
                    ch = grp * 4 + i
                    xb_i = SB2[:, XBB + buf * 2048 + i * 512: XBB + buf * 2048 + (i + 1) * 512]
                    tmp_i = SB2[:, SQB + i * 512: SQB + (i + 1) * 512]
                    tch = AB[:, tdest + ch * 512: tdest + (ch + 1) * 512]
                    run_ser("v", "SD", lambda e, a=tmp_i, b=xb_i:
                            e.tensor_tensor(a, b, PF[:, :], ALU.add))
                    run_ser("v", "SD", lambda e, a=tch, b=tmp_i:
                            e.tensor_tensor(a, b, PE_[:, :], ALU.mult))
                v_apply[grp] = cnt["SD"]
            set_led("bank_PE_", "SD", v_apply[7])
            set_led("bank_PF", "SD", v_apply[7])
            set_led("xb", "SD", v_apply[7])
            set_led("sq", "SD", v_apply[7])
            return v_apply[7]

        def emit_lora_loads(pref):
            wait_led("g", "WSL0")
            wait_led("g", "WSL1")
            wt("g", "GL", cnt["GL"])
            pref["wq0"] = dma("g", "GL", WSL[:, 0:4096], wqd[:, 0:4096])
            wt("g", "GL1", cnt["GL1"])
            pref["wq1"] = dma("g", "GL1", WSL[:, 4096:8192], wqd[:, 4096:8192])

        def emit_lora(st, tbase, v_apply, pref):
            def t_ch(ch):
                return AB[:, tbase + ch * 512: tbase + (ch + 1) * 512]

            wt("t", "GL", pref["wq0"])
            wt("t", "SD", v_apply)
            wait_led("t", "bank_PA")
            wait_led("t", "bank_PB")
            v_qh1 = 0
            for ich in range(32):
                if ich == 16:
                    wt("t", "GL1", pref["wq1"])
                for ch2, bank in ((0, PA), (1, PB)):
                    lh = WSL[:, ich * 256 + ch2 * 128: ich * 256 + (ch2 + 1) * 128]
                    run("t", lambda e, l=lh, r=t_ch(ich), s0=(ich == 0), s1=(ich == 31),
                        b=bank: e.matmul(b[:, :], l, r, start=s0, stop=s1),
                        "SP" if (ich in (15, 31) and ch2 == 1) else None)
                if ich == 15:
                    v_qh1 = cnt["SP"]
            v_qmm = cnt["SP"]
            wt("g", "SP", v_qh1)
            wt("g", "GL", cnt["GL"])
            wk0 = dma("g", "GL", WSL[:, 0:4096], wkd[:, 0:4096])
            wt("g", "SP", v_qmm)
            wt("g", "GL1", cnt["GL1"])
            wk1 = dma("g", "GL1", WSL[:, 4096:8192], wkd[:, 4096:8192])
            wt("s", "SP", v_qmm)
            run("s", lambda e: e.activation(SB2[:, LQ0:LQ0 + 512], PA, AF.Copy))
            run("s", lambda e: e.activation(SB2[:, LQ1:LQ1 + 512], PB, AF.Copy), "SA")
            v_qe = cnt["SA"]
            set_led("bank_PA", "SA", v_qe)
            set_led("bank_PB", "SA", v_qe)
            wt("t", "GL", wk0)
            wait_led("t", "bank_PC")
            wait_led("t", "bank_PD")
            v_kh1 = 0
            for ich in range(32):
                if ich == 16:
                    wt("t", "GL1", wk1)
                for ch2, bank in ((0, PC), (1, PD)):
                    lh = WSL[:, ich * 256 + ch2 * 128: ich * 256 + (ch2 + 1) * 128]
                    run("t", lambda e, l=lh, r=t_ch(ich), s0=(ich == 0), s1=(ich == 31),
                        b=bank: e.matmul(b[:, :], l, r, start=s0, stop=s1),
                        "SP" if (ich in (15, 31) and ch2 == 1) else None)
                if ich == 15:
                    v_kh1 = cnt["SP"]
            v_kmm = cnt["SP"]
            wt("g", "SP", v_kh1)
            wt("g", "GL", cnt["GL"])
            v_at = dma("g", "GL", WSL[:, 0:4096], atd[:, :])
            wt("s", "SP", v_kmm)
            run("s", lambda e: e.activation(SB2[:, LK0:LK0 + 512], PC, AF.Copy))
            run("s", lambda e: e.activation(SB2[:, LK1:LK1 + 512], PD, AF.Copy), "SA")
            v_ke = cnt["SA"]
            set_led("bank_PC", "SA", v_ke)
            set_led("bank_PD", "SA", v_ke)
            wt("v", "SA", v_ke)
            run("v", lambda e: e.tensor_tensor(SB2[:, LP0:LP0 + 512], SB2[:, LQ0:LQ0 + 512],
                                               SB2[:, LK0:LK0 + 512], ALU.mult))
            run("v", lambda e: e.tensor_tensor(SB2[:, LP1:LP1 + 512], SB2[:, LQ1:LQ1 + 512],
                                               SB2[:, LK1:LK1 + 512], ALU.mult), "SD")
            v_prod = cnt["SD"]
            wt("t", "SD", v_prod)
            wait_led("t", "bank_PG")
            run("t", lambda e: e.matmul(PG[0:8, :], SEL0, SB2[:, LP0:LP0 + 512],
                                        start=True, stop=False))
            run("t", lambda e: e.matmul(PG[0:8, :], SEL1, SB2[:, LP1:LP1 + 512],
                                        start=False, stop=True), "SP")
            v_s8 = cnt["SP"]
            wt("s", "SP", v_s8)
            run("s", lambda e: e.activation(SB2[0:8, LP8:LP8 + 512], PG[0:8, :], AF.Exp,
                                            scale=SCALE_LORA), "SA")
            v_p8 = cnt["SA"]
            wt("t", "SA", v_p8)
            run("t", lambda e: e.matmul(PG[64:65, :], ONES8, SB2[0:8, LP8:LP8 + 512],
                                        start=True, stop=True), "SP")
            v_z8 = cnt["SP"]
            wt("v", "SP", v_z8)
            wait_led("v", "rows")
            run("v", lambda e: e.reciprocal(M_ROW, PG[64:65, :]), "SD")
            v_zi = cnt["SD"]
            set_led("bank_PG", "SD", v_zi)
            wt("t", "SD", v_zi)
            wait_led("t", "bank_PE_")
            wait_led("t", "bank_PF")
            run("t", lambda e: e.matmul(PE_[:, :], EXP8, SB2[0:8, LP8:LP8 + 512],
                                        start=True, stop=True))
            run("t", lambda e: e.matmul(PF[:, :], ONEROW, M_ROW, start=True, stop=True), "SP")
            v_bc8 = cnt["SP"]
            set_led("rows", "SP", v_bc8)
            wt("t", "GL", v_at)
            for ich in range(32):
                lh = WSL[:, ich * 128:(ich + 1) * 128]
                run("t", lambda e, l=lh, r=t_ch(ich), s0=(ich == 0), s1=(ich == 31):
                    e.matmul(PA, l, r, start=s0, stop=s1),
                    "SP" if ich == 31 else None)
            v_h = cnt["SP"]
            set_led("WSL0", "SP", v_h)
            set_led("WSL1", "SP", v_kmm)
            wt("s", "SP", v_h)
            run("s", lambda e: e.activation(SB2[:, LH:LH + 512], PA, AF.Copy), "SA")
            v_he = cnt["SA"]
            set_led("bank_PA", "SA", v_he)
            wt("v", "SA", v_he)
            wt("v", "SP", v_bc8)
            run_ser("v", "SD", lambda e: e.tensor_tensor(SB2[:, LHW:LHW + 512], PE_[:, :],
                                                         SB2[:, LH:LH + 512], ALU.mult))
            hw2c = HW2 + st * 512
            run_ser("v", "SD", lambda e, c=hw2c: e.tensor_tensor(
                SB2[:, c:c + 512], PF[:, :], SB2[:, LHW:LHW + 512], ALU.mult))
            v_hw2 = cnt["SD"]
            set_led("bank_PE_", "SD", v_hw2)
            set_led("bank_PF", "SD", v_hw2)
            return v_hw2

        def emit_oloop(wsrc, rhs_col, resid_src, dest, ffn=True,
                       pre_waits=(), xs_raw=()):
            n_oc = 32
            slab_v = {}
            pe_v = {}
            ev = {}
            dve_v = {}
            xs_v = {}
            dma_v = {}
            w3 = wsrc.ap().rearrange("p (o co) -> p o co", co=4096)

            def load_slab(oc):
                wait_led("g", "WSL%d" % (oc % 2))
                if oc - 2 in slab_v:
                    wt("g", "GW%d" % (oc % 2), slab_v[oc - 2])
                wt("g", "GW%d" % (oc % 2), cnt["GW%d" % (oc % 2)])
                slab_v[oc] = dma("g", "GW%d" % (oc % 2),
                                 WSL[:, (oc % 2) * 4096:(oc % 2 + 1) * 4096], w3[:, oc, :])

            def load_xs(g2):
                oc, st = divmod(g2, 2)
                if g2 < 2:
                    wait_led("y", "XS")
                    for s_, v_ in xs_raw:
                        wt("y", s_, v_)
                else:
                    wt("y", "SD", dve_v[g2 - 2])
                    wt("y", "YX%d" % (g2 % 2), xs_v[g2 - 2])
                xs_v[g2] = dma("y", "YX%d" % (g2 % 2), XS[:, (g2 % 2) * 512:(g2 % 2 + 1) * 512],
                               resid_src[:, oc * 1024 + st * 512: oc * 1024 + (st + 1) * 512])

            load_slab(0)
            load_slab(1)
            load_xs(0)
            load_xs(1)
            for g2 in range(2 * n_oc):
                oc, st = divmod(g2, 2)
                bank = PA if g2 % 2 == 0 else PB
                wt("t", "GW%d" % (oc % 2), slab_v[oc])
                if g2 - 2 in ev:
                    wt("t", ev[g2 - 2][0], ev[g2 - 2][1])
                if g2 == 0:
                    for s_, v_ in pre_waits:
                        wt("t", s_, v_)
                    wait_led("t", "bank_PA")
                if g2 == 1:
                    wait_led("t", "bank_PB")
                for ich in range(32):
                    lh = WSL[:, (oc % 2) * 4096 + ich * 128:(oc % 2) * 4096 + (ich + 1) * 128]
                    rc = rhs_col(ich, st)
                    last = (ich == 31) and not ffn
                    run("t", lambda e, l=lh, c=rc, s0=(ich == 0), s1=(ich == 31 and not ffn),
                        b=bank: e.matmul(b[:, :], l, AB[:, c:c + 512], start=s0, stop=s1),
                        "SP" if last else None)
                if ffn:
                    hc = HW2 + st * 512
                    run("t", lambda e, c=hc, o=oc, b=bank: e.matmul(
                        b[:, :], BM2S[:, o * 128:(o + 1) * 128], SB2[:, c:c + 512],
                        start=False, stop=True), "SP")
                pe_v[g2] = cnt["SP"]
                if st == 1 and oc + 2 < n_oc:
                    wt("g", "SP", pe_v[g2])
                    load_slab(oc + 2)
                if ffn:
                    wt("s", "SP", pe_v[g2])
                    if g2 - 2 in dve_v:
                        wt("s", "SD", dve_v[g2 - 2])
                    else:
                        wait_led("s", "US")
                    uu = US[:, (g2 % 2) * 512:(g2 % 2 + 1) * 512]
                    run("s", lambda e, u=uu, b=bank: e.activation(
                        u, b[:, :], AF.Copy, scale=0.5), "SA")
                    ev[g2] = ("SA", cnt["SA"])
                    wt("v", "SA", ev[g2][1])
                    wt("v", "YX%d" % (g2 % 2), xs_v[g2])
                    if g2 - 2 in dma_v:
                        wt("v", "YW%d" % (g2 % 2), dma_v[g2 - 2])
                    else:
                        wait_led("v", "U2")
                    u2 = U2[:, (g2 % 2) * 512:(g2 % 2 + 1) * 512]
                    run("v", lambda e, a=u2, u=uu, o=oc,
                        x=XS[:, (g2 % 2) * 512:(g2 % 2 + 1) * 512]:
                        e.scalar_tensor_tensor(a, u, HB[:, o:o + 1], x, ALU.add, ALU.add),
                        "SD")
                    dve_v[g2] = cnt["SD"]
                else:
                    wt("v", "SP", pe_v[g2])
                    wt("v", "YX%d" % (g2 % 2), xs_v[g2])
                    if g2 - 2 in dma_v:
                        wt("v", "YW%d" % (g2 % 2), dma_v[g2 - 2])
                    else:
                        wait_led("v", "U2")
                    u2 = U2[:, (g2 % 2) * 512:(g2 % 2 + 1) * 512]
                    run("v", lambda e, a=u2, b=bank, x=XS[:, (g2 % 2) * 512:(g2 % 2 + 1) * 512]:
                        e.tensor_tensor(a, b[:, :], x, ALU.add), "SD")
                    dve_v[g2] = cnt["SD"]
                    ev[g2] = ("SD", dve_v[g2])
                if g2 + 2 < 2 * n_oc:
                    load_xs(g2 + 2)
                wt("y", "SD", dve_v[g2])
                u2 = U2[:, (g2 % 2) * 512:(g2 % 2 + 1) * 512]
                dma_v[g2] = dma("y", "YW%d" % (g2 % 2),
                                dest[:, oc * 1024 + st * 512: oc * 1024 + (st + 1) * 512], u2)
            last = 2 * n_oc - 1
            set_led("bank_PA", ev[last - 1][0], ev[last - 1][1])
            set_led("bank_PB", ev[last][0], ev[last][1])
            set_led("WSL0", "SP", pe_v[last])
            set_led("WSL1", "SP", pe_v[last])
            set_led("XS", "SD", dve_v[last])
            set_led("U2", "YW%d" % (last % 2), dma_v[last])
            if ffn:
                set_led("US", "SD", dve_v[last])
            return (("YW0", dma_v[last - 1]), ("YW1", dma_v[last])), dma_v

        def emit_ffn(resid_src, dest, xs_raw=(), src_ready=None):
            hw2v = 0
            for st in range(NST):
                pref = {}
                va = emit_ln(resid_src, st * 512, st * 16384,
                             g_hook=lambda p=pref: emit_lora_loads(p),
                             src_ready=src_ready if st == 0 else None)
                hw2v = emit_lora(st, st * 16384, va, pref)
            return emit_oloop(wffd, lambda ich, st: st * 16384 + ich * 512,
                              resid_src, dest, ffn=True,
                              pre_waits=(("SD", hw2v),), xs_raw=xs_raw)[0]

        v_x1_done = emit_ffn(xd, x1d)

        v_vtm = 0
        v_qkv_mm = 0
        for s_, v_ in v_x1_done:
            wt("g", s_, v_)
        for st in range(NST if upto >= 2 else 0):
            va = emit_ln(x1d, st * 512, T1B)

            def t1_ch(ch):
                return AB[:, T1B + ch * 512: T1B + (ch + 1) * 512]

            slab_v = {}
            ev = {}
            w3 = wqkvd.ap().rearrange("p (o co) -> p o co", co=4096)
            SLABS = [WSL[:, 0:4096], WSL[:, 4096:8192], SB2[:, 0:4096]]
            oc_order = list(range(32, 48)) + list(range(32))

            def qload(pos):
                b3 = pos % 3
                if b3 < 2:
                    wait_led("g", "WSL%d" % b3)
                else:
                    wait_led("g", "xb")
                    wait_led("g", "xb2")
                wt("g", "GW%d" % b3, cnt["GW%d" % b3])
                slab_v[pos] = dma("g", "GW%d" % b3, SLABS[b3], w3[:, oc_order[pos], :])

            qload(0)
            qload(1)
            qload(2)
            wt("t", "SD", va)
            for pos in range(48):
                oc = oc_order[pos]
                b3 = pos % 3
                bank = PA if pos % 2 == 0 else PB
                wt("t", "GW%d" % b3, slab_v[pos])
                if pos - 2 in ev:
                    wt("t", ev[pos - 2][0], ev[pos - 2][1])
                if pos == 0:
                    wait_led("t", "bank_PA")
                if pos == 1:
                    wait_led("t", "bank_PB")
                for ich in range(32):
                    lh = SLABS[b3][:, ich * 128:(ich + 1) * 128]
                    run("t", lambda e, l=lh, r=t1_ch(ich), s0=(ich == 0), s1=(ich == 31),
                        b=bank: e.matmul(b[:, :], l, r, start=s0, stop=s1),
                        "SP" if ich == 31 else None)
                v_pe = cnt["SP"]
                v_qkv_mm = v_pe
                if pos + 3 < 48:
                    wt("g", "SP", v_pe)
                    qload(pos + 3)
                wt("s", "SP", v_pe)
                if oc < 32:
                    dst = AB[:, oc * 1024 + st * 512: oc * 1024 + (st + 1) * 512]
                    run("s", lambda e, d=dst, b=bank: e.activation(d, b[:, :], AF.Copy), "SA")
                    ev[pos] = ("SA", cnt["SA"])
                elif oc < 40:
                    c = oc - 32
                    dst = AB[:, KVB + c * 1024 + st * 512: KVB + c * 1024 + (st + 1) * 512]
                    run("s", lambda e, d=dst, b=bank: e.activation(d, b[:, :], AF.Copy), "SA")
                    ev[pos] = ("SA", cnt["SA"])
                else:
                    h = oc - 40
                    wait_led("s", "VST")
                    run("s", lambda e, b=bank: e.activation(
                        SB2[:, VSTC:VSTC + 512], b[:, :], AF.Copy), "SA")
                    v_vst = cnt["SA"]
                    ev[pos] = ("SA", v_vst)
                    wt("t", "SA", v_vst)
                    wait_led("t", "bank_PT8")
                    for tl in range(4):
                        run("t", lambda e, i=tl: e.transpose(
                            PT8[:, i * 128:(i + 1) * 128],
                            SB2[:, VSTC + i * 128: VSTC + (i + 1) * 128], IDEN),
                            "SP" if tl == 3 else None)
                    v_tp = cnt["SP"]
                    set_led("VST", "SP", v_tp)
                    wt("v", "SP", v_tp)
                    base = VTB + st * 4 * 1024
                    dstv = AB[:, base: base + 4 * 1024].rearrange(
                        "p (t x) -> p t x", x=1024)[:, :, h * 128:(h + 1) * 128]
                    srcv = PT8[:, 0:512].rearrange("p (t x) -> p t x", x=128)
                    run("v", lambda e, d=dstv, s=srcv: e.tensor_copy(d, s), "SD")
                    v_vtm = cnt["SD"]
                    set_led("bank_PT8", "SD", v_vtm)
                if st == 1 and pos == 15 and upto >= 2:
                    wt("y", "SA", cnt["SA"])
                    wt("y", "SD", v_vtm)
                    for c in range(8):
                        dma("y", "YP", kvpay[c * 128:(c + 1) * 128, :],
                            AB[:, KVB + c * 1024: KVB + (c + 1) * 1024])
                    for tl in range(8):
                        dma("y", "YP", kvpay[1024 + tl * 128: 1024 + (tl + 1) * 128, :],
                            AB[:, VTB + tl * 1024: VTB + (tl + 1) * 1024])
                    v_pay = cnt["YP"]
                    wt("g", "YP", v_pay)
                    run("g", lambda e: e.collective_compute(
                        "AllGather", ALU.bypass,
                        replica_groups=[[0, 1], [2, 3], [4, 5], [6, 7]],
                        ins=[kvpay.ap().opt()], outs=[kvgath.ap().opt()]), "SCC")
            set_led("bank_PA", ev[46][0], ev[46][1])
            set_led("bank_PB", ev[47][0], ev[47][1])
            set_led("WSL0", "SP", cnt["SP"])
            set_led("WSL1", "SP", cnt["SP"])
            set_led("xb2", "SP", cnt["SP"])
        v_qk_done = cnt["SA"]

        if upto < 2:
            wt("y", "YW0", cnt["YW0"])
            wt("y", "YW1", cnt["YW1"])
            return _finish(nc, stack, ops, block_emit)
        wt("y", "SCC", 1)
        wt("y", "SP", v_qkv_mm)
        for s2 in range(2):
            for hh in range(8):
                src = kvgath[s2 * 2048 + hh * 128: s2 * 2048 + (hh + 1) * 128, :].rearrange(
                    "d (t k) -> d t k", k=128)
                dst = AB[:, T1B + hh * 2048: T1B + (hh + 1) * 2048].rearrange(
                    "p (t x) -> p t x", x=256)[:, :, s2 * 128:(s2 + 1) * 128]
                dma("y", "YG", dst, src)
        for s2 in range(2):
            for tl in range(8):
                src = kvgath[s2 * 2048 + 1024 + tl * 128: s2 * 2048 + 1024 + (tl + 1) * 128, :]
                p = 2 * tl + s2
                dma("y", "YG", AB[:, KVB + p * 1024: KVB + (p + 1) * 1024], src)
        v_kv_in = cnt["YG"]

        if upto < 3:
            wt("y", "YG", cnt["YG"])
            return _finish(nc, stack, ops, block_emit)
        PG2 = PT8.bitcast(F32)
        exp_v = {}
        zav_v = {}
        bcz_v = {}
        bcze_v = {}
        qa_v = {}
        wait_led("v", "bank_PA")
        wait_led("v", "bank_PB")
        wait_led("v", "bank_PC")
        wait_led("v", "bank_PD")
        run("v", lambda e: e.memset(P01[:, :], 0.0))
        run("v", lambda e: e.memset(P23[:, :], 0.0), "SD")
        v_stinit = cnt["SD"]
        STREAMS = (
            dict(sid=0, q2=0, NB=4, ST=P01, pbase=0, zrow=M_ROW),
            dict(sid=1, q2=1, NB=8, ST=P23, pbase=2048, zrow=X_ROW),
        )
        first_st = True
        for g in range(32):
            hh = g // 4

            def a_stb(S, k):
                ST = S["ST"]
                qs = max(k - S["q2"] * 4, 0) * 128
                qbase = g * 1024 + S["q2"] * 512
                if (g, S["sid"], k - 1) in exp_v:
                    wt("t", "SA", exp_v[(g, S["sid"], k - 1)])
                elif k == 0 and (g - 1, S["sid"], S["NB"] - 1) in exp_v:
                    wt("t", "SA", exp_v[(g - 1, S["sid"], S["NB"] - 1)])
                for ti in range(2):
                    t = 2 * k + ti
                    kcol = T1B + hh * 2048 + t * 128
                    dst = ST[:, ti * 512 + qs: (ti + 1) * 512]
                    run("t", lambda e, kk=kcol, q=qbase + qs, w=512 - qs, d=dst:
                        e.matmul(d, AB[:, kk:kk + 128], AB[:, q:q + w],
                                 start=True, stop=True), "SP" if ti == 1 else None)
                return cnt["SP"]

            def a_exp(S, k, st_v):
                ST = S["ST"]
                qs = max(k - S["q2"] * 4, 0) * 128
                pb = S["pbase"] + (k % 2) * 1024
                wt("s", "SP", st_v)
                if (g, S["sid"], k - 2) in zav_v:
                    wt("s", "SP", zav_v[(g, S["sid"], k - 2)])
                elif k < 2 and (g - 1, S["sid"], S["NB"] - 1) in zav_v:
                    wt("s", "SP", zav_v[(g - 1, S["sid"], S["NB"] - 1)])
                run("s", lambda e, o=qs, p=pb: e.activation(
                    PT[:, p + o:p + 1024], ST[:, o:1024], AF.Exp,
                    scale=SCALE_ATTN), "SA")
                exp_v[(g, S["sid"], k)] = cnt["SA"]
                if k >= S["q2"] * 4:
                    wt("v", "SA", exp_v[(g, S["sid"], k)])
                    for ti in range(2):
                        t = 2 * k + ti
                        pcol = pb + ti * 512 + qs
                        run("v", lambda e, p=pcol, t_=t: e.tensor_tensor(
                            PT[:, p:p + 128], PT[:, p:p + 128],
                            MSK[:, t_ * 128:(t_ + 1) * 128], ALU.mult),
                            "SD" if ti == 1 else None)
                    return cnt["SD"]
                return None

            def a_zav(S, k, mv):
                qs = max(k - S["q2"] * 4, 0) * 128
                pb = S["pbase"] + (k % 2) * 1024
                av = PE_ if S["sid"] == 0 else PF
                zt = PG if S["sid"] == 0 else PG2
                T2 = 2 * S["NB"]
                if mv:
                    wt("t", "SD", mv)
                else:
                    wt("t", "SA", exp_v[(g, S["sid"], k)])
                if k == 0:
                    if (g - 1, S["sid"]) in bcze_v:
                        wt("t", "SA", bcze_v[(g - 1, S["sid"])])
                    elif S["sid"] == 1:
                        wait_led("t", "bank_PT8")
                    else:
                        wait_led("t", "bank_PG")
                    if (g - 1, S["sid"]) in qa_v:
                        wt("t", "SD", qa_v[(g - 1, S["sid"])])
                    else:
                        wait_led("t", "bank_PE_" if S["sid"] == 0 else "bank_PF")
                for ti in range(2):
                    t = 2 * k + ti
                    pcol = pb + ti * 512 + qs
                    pw = 512 - qs
                    s0 = (t == 0)
                    s1 = (t == T2 - 1)
                    run("t", lambda e, o=qs, p=pcol, w=pw, s0_=s0, s1_=s1, z=zt:
                        e.matmul(z[0:1, o:512], ONES, PT[:, p:p + w],
                                 start=s0_, stop=s1_))
                    vcol = KVB + t * 1024 + hh * 128
                    run("t", lambda e, v=vcol, o=qs, p=pcol, w=pw, s0_=s0, s1_=s1,
                        b=av: e.matmul(b[:, o:512], AB[:, v:v + 128], PT[:, p:p + w],
                                       start=s0_, stop=s1_),
                        "SP" if ti == 1 else None)
                return cnt["SP"]

            def a_tail(S):
                av = PE_ if S["sid"] == 0 else PF
                zt = PG if S["sid"] == 0 else PG2
                zr = S["zrow"]
                qbase = g * 1024 + S["q2"] * 512
                wt("v", "SP", zav_v[(g, S["sid"], S["NB"] - 1)])
                if (g - 1, S["sid"]) in bcz_v:
                    wt("v", "SP", bcz_v[(g - 1, S["sid"])])
                else:
                    wait_led("v", "rows")
                run("v", lambda e, z=zt, r=zr: e.reciprocal(r, z[0:1, 0:512]), "SD")
                v_zi = cnt["SD"]
                wt("t", "SD", v_zi)
                run("t", lambda e, z=zt, r=zr: e.matmul(
                    z[:, 0:512], ONEROW, r, start=True, stop=True), "SP")
                bcz_v[(g, S["sid"])] = cnt["SP"]
                wt("s", "SP", bcz_v[(g, S["sid"])])
                if (g - 1, S["sid"]) in qa_v:
                    wt("s", "SD", qa_v[(g - 1, S["sid"])])
                else:
                    wait_led("s", "US")
                uu = US[:, S["sid"] * 512:(S["sid"] + 1) * 512]
                run("s", lambda e, u=uu, z=zt: e.activation(u, z[:, 0:512], AF.Copy), "SA")
                bcze_v[(g, S["sid"])] = cnt["SA"]
                wt("v", "SA", bcze_v[(g, S["sid"])])
                run("v", lambda e, q=qbase, b=av, u=uu: e.tensor_tensor(
                    AB[:, q:q + 512], b[:, :], u, ALU.mult), "SD")
                qa_v[(g, S["sid"])] = cnt["SD"]

            if first_st:
                wt("t", "YG", v_kv_in)
                wt("t", "SD", v_stinit)
                first_st = False
            A, B = STREAMS
            mv = {}
            stv = {}
            for k in range(9):
                for S in (A, B):
                    if k < S["NB"]:
                        stv[(S["sid"], k)] = a_stb(S, k)
                for S in (A, B):
                    if 1 <= k <= S["NB"]:
                        zav_v[(g, S["sid"], k - 1)] = a_zav(S, k - 1, mv[(S["sid"], k - 1)])
                for S in (A, B):
                    if k < S["NB"]:
                        mv[(S["sid"], k)] = a_exp(S, k, stv[(S["sid"], k)])
            a_tail(A)
            a_tail(B)
        v_attn_done = qa_v[(31, 1)]
        set_led("bank_PA", "SA", exp_v[(31, 1, 7)])
        set_led("bank_PB", "SA", exp_v[(31, 1, 7)])
        set_led("bank_PC", "SA", exp_v[(31, 1, 7)])
        set_led("bank_PD", "SA", exp_v[(31, 1, 7)])
        set_led("bank_PE_", "SD", qa_v[(31, 0)])
        set_led("bank_PF", "SD", qa_v[(31, 1)])
        set_led("bank_PG", "SA", bcze_v[(31, 0)])
        set_led("bank_PT8", "SA", bcze_v[(31, 1)])
        set_led("rows", "SP", max(bcz_v[(31, 0)], bcz_v[(31, 1)]))
        set_led("US", "SD", qa_v[(31, 1)])

        if debug:
            wt("y", "SD", v_attn_done)
            dma("y", "YP", qadump.ap(), AB[:, 0:32768])

        if upto < 4:
            if debug:
                wt("y", "YP", cnt["YP"])
            wt("y", "YW0", cnt["YW0"])
            wt("y", "YW1", cnt["YW1"])
            return _finish(nc, stack, ops, block_emit)
        v_x2_done, wr_x2 = emit_oloop(wdd, lambda ich, st: ich * 1024 + st * 512, x1d, x2d,
                                      ffn=False, pre_waits=(("SD", v_attn_done),),
                                      xs_raw=v_x1_done)

        def x2_ready(grp):
            oc = grp * 4 + 3
            return (("YW0", wr_x2[2 * oc]), ("YW1", wr_x2[2 * oc + 1]))

        if upto < 5:
            wt("y", "YW0", cnt["YW0"])
            wt("y", "YW1", cnt["YW1"])
            return _finish(nc, stack, ops, block_emit)
        if debug:
            wt("v", "YP", cnt["YP"])
        emit_ffn(x2d, outd, xs_raw=v_x2_done, src_ready=x2_ready)
        wt("y", "YW0", cnt["YW0"])
        wt("y", "YW1", cnt["YW1"])
        if debug:
            wt("y", "YP", cnt["YP"])

        return _finish(nc, stack, ops, block_emit)


def block_emit(nc, ops):
    with nc.Block() as block:
        @block.tensor
        def _(e):
            for f in ops["t"]:
                f(e)

        @block.scalar
        def _(e):
            for f in ops["s"]:
                f(e)

        @block.vector
        def _(e):
            for f in ops["v"]:
                f(e)

        @block.gpsimd
        def _(e):
            for f in ops["g"]:
                f(e)

        @block.sync
        def _(e):
            for f in ops["y"]:
                f(e)


def _finish(nc, stack, ops, emit):
    try:
        emit(nc, ops)
    finally:
        stack.close()
    return nc


def _tile_lhs(w):
    O, I = w.shape
    noc, nch = O // 128, I // 128
    return np.ascontiguousarray(
        w.reshape(noc, 128, nch, 128).transpose(3, 0, 2, 1).reshape(128, noc * nch * 128))


def _tile_moving(wt):
    I, O = wt.shape
    return np.ascontiguousarray(
        wt.reshape(I // 128, 128, O).transpose(1, 0, 2).reshape(128, (I // 128) * O))


def _tile_x(tok_major):
    f = np.ascontiguousarray(tok_major.T)
    return np.ascontiguousarray(f.reshape(32, 128, 1024).transpose(1, 0, 2).reshape(128, 32768))


def _untile_x(arr):
    f = arr.reshape(128, 32, 1024).transpose(1, 0, 2).reshape(4096, 1024)
    return np.ascontiguousarray(f.T)


def _consts():
    cb = np.zeros((128, 512), np.float32)
    cb[:, 0] = 1.0
    for e in range(4):
        cb[e * 32:(e + 1) * 32, 1 + e] = 1.0
        cb[e * 32:(e + 1) * 32, 9 + 4 + e] = 1.0
    for e in range(8):
        cb[e, 17 + e * 16: 17 + (e + 1) * 16] = 1.0
    cb[:, 145:273] = np.eye(128, dtype=np.float32)
    cb[0, 273:401] = 1.0
    cf = np.zeros((128, 132), np.float32)
    cf[0, 0:128] = 1.0
    cf[0, 128] = 1e-5
    return cb.astype(BF), cf


def _mask(par):
    m = np.zeros((128, 2048), np.float32)
    own = OWN[par]
    for t in range(16):
        pb = PERM_ABS[t]
        ab = own[t // 2]
        if pb < ab:
            blk = np.ones((128, 128), np.float32)
        elif pb > ab:
            blk = np.zeros((128, 128), np.float32)
        else:
            blk = np.triu(np.ones((128, 128), np.float32))
        m[:, t * 128:(t + 1) * 128] = blk
    return m.astype(BF)


def kernel(L0_attn_q, L0_attn_w_qkv, L0_attn_w_dense, L0_ffn_W, L0_ffn_b,
           L0_ffn_Wq, L0_ffn_Wk, L0_ffn_A, L0_ffn_Bm, _debug=False):
    x = np.asarray(L0_attn_q, np.float32)
    wff_t = _tile_lhs(np.asarray(L0_ffn_W, np.float32))
    wqkv_t = _tile_lhs(np.asarray(L0_attn_w_qkv, np.float32))
    wd_t = _tile_lhs(np.asarray(L0_attn_w_dense, np.float32))
    wq_t = _tile_moving(np.ascontiguousarray(np.asarray(L0_ffn_Wq, np.float32).T))
    wk_t = _tile_moving(np.ascontiguousarray(np.asarray(L0_ffn_Wk, np.float32).T))
    a2 = np.asarray(L0_ffn_A, np.float32).reshape(128, 4096)
    at_t = _tile_moving(np.ascontiguousarray(a2.T))
    bm2_t = np.ascontiguousarray(np.asarray(L0_ffn_Bm, np.float32).transpose(0, 2, 1)
                                 .reshape(128, 4096))
    b2_t = np.ascontiguousarray(np.asarray(L0_ffn_b, np.float32).reshape(32, 128).T)
    cb, cf = _consts()
    masks = [_mask(0), _mask(1)]

    in_maps = []
    for c in range(8):
        seq, par = c // 2, c % 2
        rows = np.concatenate([x[seq, ab * 128:(ab + 1) * 128, :] for ab in OWN[par]], axis=0)
        in_maps.append({
            "x": _tile_x(rows), "wff": wff_t, "wqkv": wqkv_t, "wd": wd_t,
            "wq": wq_t, "wk": wk_t, "at": at_t, "bm2": bm2_t, "b2": b2_t,
            "cb": cb, "cf": cf, "mask": masks[par],
        })

    nc = build_nc(debug=_debug)
    res = run_bass_kernel_spmd(nc, in_maps, core_ids=list(range(8)))

    y = np.empty((4, 2048, 4096), np.float32)
    for c in range(8):
        seq, par = c // 2, c % 2
        toks = _untile_x(res.results[c]["out"])
        for li, ab in enumerate(OWN[par]):
            y[seq, ab * 128:(ab + 1) * 128, :] = toks[li * 128:(li + 1) * 128]
    if _debug:
        return y, res
    return y


# version 14
# speedup vs baseline: 1.0530x; 1.0084x over previous
import numpy as np
import ml_dtypes
import concourse.bass as bass
import concourse.mybir as mybir
from concourse.bass_utils import run_bass_kernel_spmd

F32 = mybir.dt.float32
BF16 = mybir.dt.bfloat16
AF = mybir.ActivationFunctionType
ALU = mybir.AluOpType
BF = ml_dtypes.bfloat16

D = 4096
NCH = 32
SC = 1024
NST = 2
SCALE_ATTN = float(1.0 / np.sqrt(128.0))
SCALE_LORA = float(1.0 / np.sqrt(32.0))
INV_D = float(1.0 / 4096.0)

OWN = [[0, 3, 4, 7, 8, 11, 12, 15], [1, 2, 5, 6, 9, 10, 13, 14]]
PERM_ABS = [0, 1, 3, 2, 4, 5, 7, 6, 8, 9, 11, 10, 12, 13, 15, 14]

T1B = 32768
KVB = 49152
VTB = 57344
XBB = 0
SQB = 4096
LQ0 = 6144; LQ1 = 6656; LK0 = 7168; LK1 = 7680
LP0 = 8192; LP1 = 8704; LP8 = 9216; LH = 9728; LHW = 10240
HW2 = 10752
VSTC = 11776
RM = 0; RX = 512; RT = 1024


def build_nc(debug=False, upto=9):
    nc = bass.Bass("TRN2", target_bir_lowering=False, debug=False)

    xd = nc.dram_tensor("x", [128, NCH * SC], F32, kind="ExternalInput")
    wffd = nc.dram_tensor("wff", [128, 32 * 4096], F32, kind="ExternalInput")
    wqkvd = nc.dram_tensor("wqkv", [128, 48 * 4096], F32, kind="ExternalInput")
    wdd = nc.dram_tensor("wd", [128, 32 * 4096], F32, kind="ExternalInput")
    wqd = nc.dram_tensor("wq", [128, 32 * 256], F32, kind="ExternalInput")
    wkd = nc.dram_tensor("wk", [128, 32 * 256], F32, kind="ExternalInput")
    atd = nc.dram_tensor("at", [128, 32 * 128], F32, kind="ExternalInput")
    bm2d = nc.dram_tensor("bm2", [128, 4096], F32, kind="ExternalInput")
    b2d = nc.dram_tensor("b2", [128, 32], F32, kind="ExternalInput")
    cbd = nc.dram_tensor("cb", [128, 512], BF16, kind="ExternalInput")
    cfd = nc.dram_tensor("cf", [128, 132], F32, kind="ExternalInput")
    maskd = nc.dram_tensor("mask", [128, 2048], BF16, kind="ExternalInput")

    okind = dict(kind="ExternalOutput") if debug else {}
    x1d = nc.dram_tensor("x1d", [128, NCH * SC], F32, **okind)
    x2d = nc.dram_tensor("x2d", [128, NCH * SC], F32, **okind)
    kvpay = nc.dram_tensor("kvpay", [2048, 1024], BF16)
    kvgath = nc.dram_tensor("kvgath", [4096, 1024], BF16)
    outd = nc.dram_tensor("out", [128, NCH * SC], F32, kind="ExternalOutput")
    if debug:
        qadump = nc.dram_tensor("qadump", [128, 32768], BF16, kind="ExternalOutput")

    import contextlib
    with contextlib.ExitStack() as stack:
        en = stack.enter_context
        AB = en(nc.sbuf_tensor("AB", [128, 65536], BF16))
        WSL = en(nc.sbuf_tensor("WSL", [128, 8192], BF16))
        BM2S = en(nc.sbuf_tensor("BM2S", [128, 4096], BF16))
        SB2 = en(nc.sbuf_tensor("SB2", [128, 12288], BF16))
        PT = en(nc.sbuf_tensor("PTS", [128, 4096], BF16))
        MSK = en(nc.sbuf_tensor("MSK", [128, 2048], BF16))
        CBS = en(nc.sbuf_tensor("CBS", [128, 512], BF16))
        CFS = en(nc.sbuf_tensor("CFS", [128, 132], F32))
        HB = en(nc.sbuf_tensor("HB", [128, 32], F32))
        XS = en(nc.sbuf_tensor("XS", [128, 1024], F32))
        US = en(nc.sbuf_tensor("US", [128, 1024], F32))
        U2 = en(nc.sbuf_tensor("U2", [128, 1024], F32))
        ROWS = en(nc.sbuf_tensor("ROWS", [1, 1536], F32))

        P01 = en(nc.psum_tensor("P01", [128, 1024], F32))
        P23 = en(nc.psum_tensor("P23", [128, 1024], F32))
        P45 = en(nc.psum_tensor("P45", [128, 1024], F32))
        PG = en(nc.psum_tensor("PG", [128, 512], F32))
        PT8 = en(nc.psum_tensor("PT8", [128, 1024], BF16))
        PA = P01[:, 0:512]
        PB = P01[:, 512:1024]
        PC = P23[:, 0:512]
        PD = P23[:, 512:1024]
        PE_ = P45[:, 0:512]
        PF = P45[:, 512:1024]

        SEM_NAMES = ("SP", "SA", "SD", "SCC", "GC", "GW0", "GW1", "GW2", "GL", "GL1", "GX0", "GX1",
                     "YC", "YX0", "YX1", "YW0", "YW1", "YP", "YG")
        SEM = {n: en(nc.semaphore(n)) for n in SEM_NAMES}
        ops = {k: [] for k in ("t", "s", "v", "g", "y")}
        cnt = {n: 0 for n in SEM_NAMES}

        def wt(eng, sname, val):
            if val > 0:
                ops[eng].append(lambda e, _s=SEM[sname], _v=val: e.wait_ge(_s, _v))

        def run(eng, builder, inc=None, n=1):
            if inc is None:
                ops[eng].append(builder)
            else:
                bump(inc, n)
                ops[eng].append(lambda e, _b=builder, _s=SEM[inc], _n=n: _b(e).then_inc(_s, _n))

        def bump(sname, n=1):
            cnt[sname] += n
            return cnt[sname]

        def dma(eng, sname, out_ap, in_ap):
            run(eng, lambda e, o=out_ap, i=in_ap: e.dma_start(out=o, in_=i), sname, 16)
            return cnt[sname]

        led = {}

        def set_led(key, sname, val):
            led[key] = (sname, val)

        def wait_led(eng, key):
            if key in led:
                s, v = led[key]
                wt(eng, s, v)

        def run_ser(eng, sem, builder):
            wt(eng, sem, cnt[sem])
            run(eng, builder, sem)

        ONES = CBS[:, 0:1]
        ONES8 = CBS[0:8, 0:1]
        SEL0 = CBS[:, 1:9]
        SEL1 = CBS[:, 9:17]
        EXP8 = CBS[0:8, 17:145]
        IDEN = CBS[:, 145:273]
        ONEROW = CFS[0:1, 0:128]
        EPS = CFS[0:1, 128:129]
        M_ROW = ROWS[0:1, RM:RM + 512]
        X_ROW = ROWS[0:1, RX:RX + 512]
        T_ROW = ROWS[0:1, RT:RT + 512]

        dma("g", "GC", CBS[:, :], cbd[:, :])
        dma("y", "YC", CFS[:, :], cfd[:, :])
        dma("g", "GC", MSK[:, :], maskd[:, :])
        dma("y", "YC", HB[:, :], b2d[:, :])
        dma("g", "GC", BM2S[:, :], bm2d[:, :])
        v_consts_g = cnt["GC"]
        v_consts_x = cnt["YC"]
        wt("v", "YC", v_consts_x)
        run("v", lambda e: e.tensor_scalar_mul(HB[:, :], HB[:, :], 0.5), "SD")
        v_hb = cnt["SD"]
        wt("t", "GC", v_consts_g)
        wt("s", "SD", v_hb)
        wt("s", "YC", v_consts_x)
        wt("v", "GC", v_consts_g)

        def emit_ln(src, s_off, tdest, g_hook=None, src_ready=None):
            src3 = src.ap().rearrange("p (c s) -> p c s", s=SC)
            pe_stats = {}
            for grp in range(8):
                buf = grp % 2
                if grp < 2:
                    wait_led("g", "xb")
                    wait_led("g", "xb2")
                else:
                    wt("g", "SP", pe_stats[grp - 2])
                if src_ready is not None:
                    for s_, v_ in src_ready(grp):
                        wt("g", s_, v_)
                xbv = SB2[:, XBB + buf * 2048: XBB + (buf + 1) * 2048].rearrange(
                    "p (c s) -> p c s", s=512)
                xch = "GX%d" % buf
                wt("g", xch, cnt[xch])
                vload = dma("g", xch, xbv, src3[:, grp * 4:(grp + 1) * 4, s_off:s_off + 512])
                wt("v", xch, vload)
                if grp - 1 in pe_stats:
                    wt("v", "SP", pe_stats[grp - 1])
                elif grp == 0:
                    wait_led("v", "sq")
                for i in range(4):
                    xb_i = SB2[:, XBB + buf * 2048 + i * 512: XBB + buf * 2048 + (i + 1) * 512]
                    sq_i = SB2[:, SQB + i * 512: SQB + (i + 1) * 512]
                    run("v", lambda e, a=sq_i, b=xb_i: e.tensor_tensor(a, b, b, ALU.mult),
                        "SD" if i == 3 else None)
                v_sq = cnt["SD"]
                wt("t", xch, vload)
                if grp == 0:
                    wait_led("t", "bank_PG")
                    wait_led("t", "bank_PC")
                for i in range(4):
                    xb_i = SB2[:, XBB + buf * 2048 + i * 512: XBB + buf * 2048 + (i + 1) * 512]
                    st0 = (grp == 0 and i == 0)
                    sp1 = (grp == 7 and i == 3)
                    run("t", lambda e, r=xb_i, s0=st0, s1=sp1: e.matmul(
                        PG[0:1, :], ONES, r, start=s0, stop=s1))
                wt("t", "SD", v_sq)
                for i in range(4):
                    sq_i = SB2[:, SQB + i * 512: SQB + (i + 1) * 512]
                    st0 = (grp == 0 and i == 0)
                    sp1 = (grp == 7 and i == 3)
                    run("t", lambda e, r=sq_i, s0=st0, s1=sp1: e.matmul(
                        PC[0:1, :], ONES, r, start=s0, stop=s1),
                        "SP" if i == 3 else None)
                pe_stats[grp] = cnt["SP"]
            wt("v", "SP", pe_stats[7])
            wait_led("v", "rows")
            run_ser("v", "SD", lambda e: e.tensor_scalar_mul(M_ROW, PG[0:1, :], INV_D))
            run_ser("v", "SD", lambda e: e.tensor_tensor(T_ROW, M_ROW, M_ROW, ALU.mult))
            run_ser("v", "SD", lambda e: e.scalar_tensor_tensor(
                X_ROW, PC[0:1, :], INV_D, T_ROW, ALU.mult, ALU.subtract))
            v_var = cnt["SD"]
            set_led("bank_PG", "SD", v_var)
            set_led("bank_PC", "SD", v_var)
            wt("s", "SD", v_var)
            run_ser("s", "SA", lambda e: e.activation(T_ROW, X_ROW, AF.Ln, bias=EPS, scale=1.0))
            run_ser("s", "SA", lambda e: e.activation(X_ROW, T_ROW, AF.Exp, scale=-0.5))
            v_rstd = cnt["SA"]
            wt("v", "SA", v_rstd)
            run("v", lambda e: e.scalar_tensor_tensor(
                T_ROW, M_ROW, -1.0, X_ROW, ALU.mult, ALU.mult), "SD")
            v_negmr = cnt["SD"]
            wt("t", "SD", v_negmr)
            wait_led("t", "bank_PE_")
            wait_led("t", "bank_PF")
            run("t", lambda e: e.matmul(PE_[:, :], ONEROW, X_ROW, start=True, stop=True))
            run("t", lambda e: e.matmul(PF[:, :], ONEROW, T_ROW, start=True, stop=True), "SP")
            v_bc = cnt["SP"]
            set_led("rows", "SP", v_bc)
            if g_hook is not None:
                g_hook()
            v_apply = {}
            for grp in range(8):
                buf = grp % 2
                if grp < 2:
                    wt("g", "SP", pe_stats[7])
                else:
                    wt("g", "SD", v_apply[grp - 2])
                xbv = SB2[:, XBB + buf * 2048: XBB + (buf + 1) * 2048].rearrange(
                    "p (c s) -> p c s", s=512)
                xch = "GX%d" % buf
                wt("g", xch, cnt[xch])
                vload = dma("g", xch, xbv, src3[:, grp * 4:(grp + 1) * 4, s_off:s_off + 512])
                wt("v", xch, vload)
                if grp == 0:
                    wt("v", "SP", v_bc)
                for i in range(4):
                    ch = grp * 4 + i
                    xb_i = SB2[:, XBB + buf * 2048 + i * 512: XBB + buf * 2048 + (i + 1) * 512]
                    tmp_i = SB2[:, SQB + i * 512: SQB + (i + 1) * 512]
                    tch = AB[:, tdest + ch * 512: tdest + (ch + 1) * 512]
                    run_ser("v", "SD", lambda e, a=tmp_i, b=xb_i:
                            e.tensor_tensor(a, b, PF[:, :], ALU.add))
                    run_ser("v", "SD", lambda e, a=tch, b=tmp_i:
                            e.tensor_tensor(a, b, PE_[:, :], ALU.mult))
                v_apply[grp] = cnt["SD"]
            set_led("bank_PE_", "SD", v_apply[7])
            set_led("bank_PF", "SD", v_apply[7])
            set_led("xb", "SD", v_apply[7])
            set_led("sq", "SD", v_apply[7])
            return v_apply

        def emit_lora_loads(pref):
            wait_led("g", "WSL0")
            wait_led("g", "WSL1")
            wt("g", "GL", cnt["GL"])
            pref["wq0"] = dma("g", "GL", WSL[:, 0:4096], wqd[:, 0:4096])
            wt("g", "GL1", cnt["GL1"])
            pref["wq1"] = dma("g", "GL1", WSL[:, 4096:8192], wqd[:, 4096:8192])

        def emit_lora(st, tbase, v_apply, pref):
            def t_ch(ch):
                return AB[:, tbase + ch * 512: tbase + (ch + 1) * 512]

            wt("t", "GL", pref["wq0"])
            wait_led("t", "bank_PA")
            wait_led("t", "bank_PB")
            v_qh1 = 0
            for ich in range(32):
                if ich % 4 == 0:
                    wt("t", "SD", v_apply[ich // 4])
                if ich == 16:
                    wt("t", "GL1", pref["wq1"])
                for ch2, bank in ((0, PA), (1, PB)):
                    lh = WSL[:, ich * 256 + ch2 * 128: ich * 256 + (ch2 + 1) * 128]
                    run("t", lambda e, l=lh, r=t_ch(ich), s0=(ich == 0), s1=(ich == 31),
                        b=bank: e.matmul(b[:, :], l, r, start=s0, stop=s1),
                        "SP" if (ich in (15, 31) and ch2 == 1) else None)
                if ich == 15:
                    v_qh1 = cnt["SP"]
            v_qmm = cnt["SP"]
            wt("g", "SP", v_qh1)
            wt("g", "GL", cnt["GL"])
            wk0 = dma("g", "GL", WSL[:, 0:4096], wkd[:, 0:4096])
            wt("g", "SP", v_qmm)
            wt("g", "GL1", cnt["GL1"])
            wk1 = dma("g", "GL1", WSL[:, 4096:8192], wkd[:, 4096:8192])
            wt("s", "SP", v_qmm)
            run("s", lambda e: e.activation(SB2[:, LQ0:LQ0 + 512], PA, AF.Copy))
            run("s", lambda e: e.activation(SB2[:, LQ1:LQ1 + 512], PB, AF.Copy), "SA")
            v_qe = cnt["SA"]
            set_led("bank_PA", "SA", v_qe)
            set_led("bank_PB", "SA", v_qe)
            wt("t", "GL", wk0)
            wait_led("t", "bank_PC")
            wait_led("t", "bank_PD")
            v_kh1 = 0
            for ich in range(32):
                if ich == 16:
                    wt("t", "GL1", wk1)
                for ch2, bank in ((0, PC), (1, PD)):
                    lh = WSL[:, ich * 256 + ch2 * 128: ich * 256 + (ch2 + 1) * 128]
                    run("t", lambda e, l=lh, r=t_ch(ich), s0=(ich == 0), s1=(ich == 31),
                        b=bank: e.matmul(b[:, :], l, r, start=s0, stop=s1),
                        "SP" if (ich in (15, 31) and ch2 == 1) else None)
                if ich == 15:
                    v_kh1 = cnt["SP"]
            v_kmm = cnt["SP"]
            wt("g", "SP", v_kh1)
            wt("g", "GL", cnt["GL"])
            v_at = dma("g", "GL", WSL[:, 0:4096], atd[:, :])
            wt("s", "SP", v_kmm)
            run("s", lambda e: e.activation(SB2[:, LK0:LK0 + 512], PC, AF.Copy))
            run("s", lambda e: e.activation(SB2[:, LK1:LK1 + 512], PD, AF.Copy), "SA")
            v_ke = cnt["SA"]
            set_led("bank_PC", "SA", v_ke)
            set_led("bank_PD", "SA", v_ke)
            wt("v", "SA", v_ke)
            run("v", lambda e: e.tensor_tensor(SB2[:, LP0:LP0 + 512], SB2[:, LQ0:LQ0 + 512],
                                               SB2[:, LK0:LK0 + 512], ALU.mult))
            run("v", lambda e: e.tensor_tensor(SB2[:, LP1:LP1 + 512], SB2[:, LQ1:LQ1 + 512],
                                               SB2[:, LK1:LK1 + 512], ALU.mult), "SD")
            v_prod = cnt["SD"]
            wt("t", "SD", v_prod)
            wait_led("t", "bank_PG")
            run("t", lambda e: e.matmul(PG[0:8, :], SEL0, SB2[:, LP0:LP0 + 512],
                                        start=True, stop=False))
            run("t", lambda e: e.matmul(PG[0:8, :], SEL1, SB2[:, LP1:LP1 + 512],
                                        start=False, stop=True), "SP")
            v_s8 = cnt["SP"]
            wt("s", "SP", v_s8)
            run("s", lambda e: e.activation(SB2[0:8, LP8:LP8 + 512], PG[0:8, :], AF.Exp,
                                            scale=SCALE_LORA), "SA")
            v_p8 = cnt["SA"]
            wt("t", "SA", v_p8)
            run("t", lambda e: e.matmul(PG[64:65, :], ONES8, SB2[0:8, LP8:LP8 + 512],
                                        start=True, stop=True), "SP")
            v_z8 = cnt["SP"]
            wt("v", "SP", v_z8)
            wait_led("v", "rows")
            run("v", lambda e: e.reciprocal(M_ROW, PG[64:65, :]), "SD")
            v_zi = cnt["SD"]
            set_led("bank_PG", "SD", v_zi)
            wt("t", "SD", v_zi)
            wait_led("t", "bank_PE_")
            wait_led("t", "bank_PF")
            run("t", lambda e: e.matmul(PE_[:, :], EXP8, SB2[0:8, LP8:LP8 + 512],
                                        start=True, stop=True))
            run("t", lambda e: e.matmul(PF[:, :], ONEROW, M_ROW, start=True, stop=True), "SP")
            v_bc8 = cnt["SP"]
            set_led("rows", "SP", v_bc8)
            wt("t", "GL", v_at)
            for ich in range(32):
                lh = WSL[:, ich * 128:(ich + 1) * 128]
                run("t", lambda e, l=lh, r=t_ch(ich), s0=(ich == 0), s1=(ich == 31):
                    e.matmul(PA, l, r, start=s0, stop=s1),
                    "SP" if ich == 31 else None)
            v_h = cnt["SP"]
            set_led("WSL0", "SP", v_h)
            set_led("WSL1", "SP", v_kmm)
            wt("s", "SP", v_h)
            run("s", lambda e: e.activation(SB2[:, LH:LH + 512], PA, AF.Copy), "SA")
            v_he = cnt["SA"]
            set_led("bank_PA", "SA", v_he)
            wt("v", "SA", v_he)
            wt("v", "SP", v_bc8)
            run_ser("v", "SD", lambda e: e.tensor_tensor(SB2[:, LHW:LHW + 512], PE_[:, :],
                                                         SB2[:, LH:LH + 512], ALU.mult))
            hw2c = HW2 + st * 512
            run_ser("v", "SD", lambda e, c=hw2c: e.tensor_tensor(
                SB2[:, c:c + 512], PF[:, :], SB2[:, LHW:LHW + 512], ALU.mult))
            v_hw2 = cnt["SD"]
            set_led("bank_PE_", "SD", v_hw2)
            set_led("bank_PF", "SD", v_hw2)
            return v_hw2

        def emit_oloop(wsrc, rhs_col, resid_src, dest, ffn=True,
                       pre_waits=(), xs_raw=()):
            n_oc = 32
            slab_v = {}
            pe_v = {}
            ev = {}
            dve_v = {}
            xs_v = {}
            dma_v = {}
            w3 = wsrc.ap().rearrange("p (o co) -> p o co", co=4096)

            def load_slab(oc):
                wait_led("g", "WSL%d" % (oc % 2))
                if oc - 2 in slab_v:
                    wt("g", "GW%d" % (oc % 2), slab_v[oc - 2])
                wt("g", "GW%d" % (oc % 2), cnt["GW%d" % (oc % 2)])
                slab_v[oc] = dma("g", "GW%d" % (oc % 2),
                                 WSL[:, (oc % 2) * 4096:(oc % 2 + 1) * 4096], w3[:, oc, :])

            def load_xs(g2):
                oc, st = divmod(g2, 2)
                if g2 < 2:
                    wait_led("y", "XS")
                    for s_, v_ in xs_raw:
                        wt("y", s_, v_)
                else:
                    wt("y", "SD", dve_v[g2 - 2])
                    wt("y", "YX%d" % (g2 % 2), xs_v[g2 - 2])
                xs_v[g2] = dma("y", "YX%d" % (g2 % 2), XS[:, (g2 % 2) * 512:(g2 % 2 + 1) * 512],
                               resid_src[:, oc * 1024 + st * 512: oc * 1024 + (st + 1) * 512])

            load_slab(0)
            load_slab(1)
            load_xs(0)
            load_xs(1)
            for g2 in range(2 * n_oc):
                oc, st = divmod(g2, 2)
                bank = PA if g2 % 2 == 0 else PB
                wt("t", "GW%d" % (oc % 2), slab_v[oc])
                if g2 - 2 in ev:
                    wt("t", ev[g2 - 2][0], ev[g2 - 2][1])
                if g2 == 0:
                    for s_, v_ in pre_waits:
                        wt("t", s_, v_)
                    wait_led("t", "bank_PA")
                if g2 == 1:
                    wait_led("t", "bank_PB")
                for ich in range(32):
                    lh = WSL[:, (oc % 2) * 4096 + ich * 128:(oc % 2) * 4096 + (ich + 1) * 128]
                    rc = rhs_col(ich, st)
                    last = (ich == 31) and not ffn
                    run("t", lambda e, l=lh, c=rc, s0=(ich == 0), s1=(ich == 31 and not ffn),
                        b=bank: e.matmul(b[:, :], l, AB[:, c:c + 512], start=s0, stop=s1),
                        "SP" if last else None)
                if ffn:
                    hc = HW2 + st * 512
                    run("t", lambda e, c=hc, o=oc, b=bank: e.matmul(
                        b[:, :], BM2S[:, o * 128:(o + 1) * 128], SB2[:, c:c + 512],
                        start=False, stop=True), "SP")
                pe_v[g2] = cnt["SP"]
                if st == 1 and oc + 2 < n_oc:
                    wt("g", "SP", pe_v[g2])
                    load_slab(oc + 2)
                if ffn:
                    wt("s", "SP", pe_v[g2])
                    if g2 - 2 in dve_v:
                        wt("s", "SD", dve_v[g2 - 2])
                    else:
                        wait_led("s", "US")
                    uu = US[:, (g2 % 2) * 512:(g2 % 2 + 1) * 512]
                    run("s", lambda e, u=uu, b=bank: e.activation(
                        u, b[:, :], AF.Copy, scale=0.5), "SA")
                    ev[g2] = ("SA", cnt["SA"])
                    wt("v", "SA", ev[g2][1])
                    wt("v", "YX%d" % (g2 % 2), xs_v[g2])
                    if g2 - 2 in dma_v:
                        wt("v", "YW%d" % (g2 % 2), dma_v[g2 - 2])
                    else:
                        wait_led("v", "U2")
                    u2 = U2[:, (g2 % 2) * 512:(g2 % 2 + 1) * 512]
                    run("v", lambda e, a=u2, u=uu, o=oc,
                        x=XS[:, (g2 % 2) * 512:(g2 % 2 + 1) * 512]:
                        e.scalar_tensor_tensor(a, u, HB[:, o:o + 1], x, ALU.add, ALU.add),
                        "SD")
                    dve_v[g2] = cnt["SD"]
                else:
                    wt("v", "SP", pe_v[g2])
                    wt("v", "YX%d" % (g2 % 2), xs_v[g2])
                    if g2 - 2 in dma_v:
                        wt("v", "YW%d" % (g2 % 2), dma_v[g2 - 2])
                    else:
                        wait_led("v", "U2")
                    u2 = U2[:, (g2 % 2) * 512:(g2 % 2 + 1) * 512]
                    run("v", lambda e, a=u2, b=bank, x=XS[:, (g2 % 2) * 512:(g2 % 2 + 1) * 512]:
                        e.tensor_tensor(a, b[:, :], x, ALU.add), "SD")
                    dve_v[g2] = cnt["SD"]
                    ev[g2] = ("SD", dve_v[g2])
                if g2 + 2 < 2 * n_oc:
                    load_xs(g2 + 2)
                wt("y", "SD", dve_v[g2])
                u2 = U2[:, (g2 % 2) * 512:(g2 % 2 + 1) * 512]
                dma_v[g2] = dma("y", "YW%d" % (g2 % 2),
                                dest[:, oc * 1024 + st * 512: oc * 1024 + (st + 1) * 512], u2)
            last = 2 * n_oc - 1
            set_led("bank_PA", ev[last - 1][0], ev[last - 1][1])
            set_led("bank_PB", ev[last][0], ev[last][1])
            set_led("WSL0", "SP", pe_v[last])
            set_led("WSL1", "SP", pe_v[last])
            set_led("XS", "SD", dve_v[last])
            set_led("U2", "YW%d" % (last % 2), dma_v[last])
            if ffn:
                set_led("US", "SD", dve_v[last])
            return (("YW0", dma_v[last - 1]), ("YW1", dma_v[last])), dma_v

        def emit_ffn(resid_src, dest, xs_raw=(), src_ready=None):
            hw2v = 0
            for st in range(NST):
                pref = {}
                va = emit_ln(resid_src, st * 512, st * 16384,
                             g_hook=lambda p=pref: emit_lora_loads(p),
                             src_ready=src_ready if st == 0 else None)
                hw2v = emit_lora(st, st * 16384, va, pref)
            return emit_oloop(wffd, lambda ich, st: st * 16384 + ich * 512,
                              resid_src, dest, ffn=True,
                              pre_waits=(("SD", hw2v),), xs_raw=xs_raw)[0]

        v_x1_done = emit_ffn(xd, x1d)

        v_vtm = 0
        v_qkv_mm = 0
        for s_, v_ in v_x1_done:
            wt("g", s_, v_)
        for st in range(NST if upto >= 2 else 0):
            va = emit_ln(x1d, st * 512, T1B)

            def t1_ch(ch):
                return AB[:, T1B + ch * 512: T1B + (ch + 1) * 512]

            slab_v = {}
            ev = {}
            w3 = wqkvd.ap().rearrange("p (o co) -> p o co", co=4096)
            SLABS = [WSL[:, 0:4096], WSL[:, 4096:8192], SB2[:, 0:4096]]
            oc_order = list(range(32, 48)) + list(range(32))

            def qload(pos):
                b3 = pos % 3
                if b3 < 2:
                    wait_led("g", "WSL%d" % b3)
                else:
                    wait_led("g", "xb")
                    wait_led("g", "xb2")
                wt("g", "GW%d" % b3, cnt["GW%d" % b3])
                slab_v[pos] = dma("g", "GW%d" % b3, SLABS[b3], w3[:, oc_order[pos], :])

            qload(0)
            qload(1)
            qload(2)
            wt("t", "SD", va[7])
            for pos in range(48):
                oc = oc_order[pos]
                b3 = pos % 3
                bank = PA if pos % 2 == 0 else PB
                wt("t", "GW%d" % b3, slab_v[pos])
                if pos - 2 in ev:
                    wt("t", ev[pos - 2][0], ev[pos - 2][1])
                if pos == 0:
                    wait_led("t", "bank_PA")
                if pos == 1:
                    wait_led("t", "bank_PB")
                for ich in range(32):
                    lh = SLABS[b3][:, ich * 128:(ich + 1) * 128]
                    run("t", lambda e, l=lh, r=t1_ch(ich), s0=(ich == 0), s1=(ich == 31),
                        b=bank: e.matmul(b[:, :], l, r, start=s0, stop=s1),
                        "SP" if ich == 31 else None)
                v_pe = cnt["SP"]
                v_qkv_mm = v_pe
                if pos + 3 < 48:
                    wt("g", "SP", v_pe)
                    qload(pos + 3)
                wt("s", "SP", v_pe)
                if oc < 32:
                    dst = AB[:, oc * 1024 + st * 512: oc * 1024 + (st + 1) * 512]
                    run("s", lambda e, d=dst, b=bank: e.activation(d, b[:, :], AF.Copy), "SA")
                    ev[pos] = ("SA", cnt["SA"])
                elif oc < 40:
                    c = oc - 32
                    dst = AB[:, KVB + c * 1024 + st * 512: KVB + c * 1024 + (st + 1) * 512]
                    run("s", lambda e, d=dst, b=bank: e.activation(d, b[:, :], AF.Copy), "SA")
                    ev[pos] = ("SA", cnt["SA"])
                else:
                    h = oc - 40
                    wait_led("s", "VST")
                    run("s", lambda e, b=bank: e.activation(
                        SB2[:, VSTC:VSTC + 512], b[:, :], AF.Copy), "SA")
                    v_vst = cnt["SA"]
                    ev[pos] = ("SA", v_vst)
                    wt("t", "SA", v_vst)
                    wait_led("t", "bank_PT8")
                    for tl in range(4):
                        run("t", lambda e, i=tl: e.transpose(
                            PT8[:, i * 128:(i + 1) * 128],
                            SB2[:, VSTC + i * 128: VSTC + (i + 1) * 128], IDEN),
                            "SP" if tl == 3 else None)
                    v_tp = cnt["SP"]
                    set_led("VST", "SP", v_tp)
                    wt("v", "SP", v_tp)
                    base = VTB + st * 4 * 1024
                    dstv = AB[:, base: base + 4 * 1024].rearrange(
                        "p (t x) -> p t x", x=1024)[:, :, h * 128:(h + 1) * 128]
                    srcv = PT8[:, 0:512].rearrange("p (t x) -> p t x", x=128)
                    run("v", lambda e, d=dstv, s=srcv: e.tensor_copy(d, s), "SD")
                    v_vtm = cnt["SD"]
                    set_led("bank_PT8", "SD", v_vtm)
                if st == 1 and pos == 15 and upto >= 2:
                    wt("y", "SA", cnt["SA"])
                    wt("y", "SD", v_vtm)
                    for c in range(8):
                        dma("y", "YP", kvpay[c * 128:(c + 1) * 128, :],
                            AB[:, KVB + c * 1024: KVB + (c + 1) * 1024])
                    for tl in range(8):
                        dma("y", "YP", kvpay[1024 + tl * 128: 1024 + (tl + 1) * 128, :],
                            AB[:, VTB + tl * 1024: VTB + (tl + 1) * 1024])
                    v_pay = cnt["YP"]
                    wt("g", "YP", v_pay)
                    run("g", lambda e: e.collective_compute(
                        "AllGather", ALU.bypass,
                        replica_groups=[[0, 1], [2, 3], [4, 5], [6, 7]],
                        ins=[kvpay.ap().opt()], outs=[kvgath.ap().opt()]), "SCC")
            set_led("bank_PA", ev[46][0], ev[46][1])
            set_led("bank_PB", ev[47][0], ev[47][1])
            set_led("WSL0", "SP", cnt["SP"])
            set_led("WSL1", "SP", cnt["SP"])
            set_led("xb2", "SP", cnt["SP"])
        v_qk_done = cnt["SA"]

        if upto < 2:
            wt("y", "YW0", cnt["YW0"])
            wt("y", "YW1", cnt["YW1"])
            return _finish(nc, stack, ops, block_emit)
        wt("y", "SCC", 1)
        for s2 in range(2):
            for tl in range(8):
                src = kvgath[s2 * 2048 + 1024 + tl * 128: s2 * 2048 + 1024 + (tl + 1) * 128, :]
                p = 2 * tl + s2
                dma("y", "YG", AB[:, KVB + p * 1024: KVB + (p + 1) * 1024], src)
        wt("y", "SP", v_qkv_mm)
        for s2 in range(2):
            for hh in range(8):
                src = kvgath[s2 * 2048 + hh * 128: s2 * 2048 + (hh + 1) * 128, :].rearrange(
                    "d (t k) -> d t k", k=128)
                dst = AB[:, T1B + hh * 2048: T1B + (hh + 1) * 2048].rearrange(
                    "p (t x) -> p t x", x=256)[:, :, s2 * 128:(s2 + 1) * 128]
                dma("y", "YG", dst, src)
        v_kv_in = cnt["YG"]

        if upto < 3:
            wt("y", "YG", cnt["YG"])
            return _finish(nc, stack, ops, block_emit)
        PG2 = PT8.bitcast(F32)
        exp_v = {}
        zav_v = {}
        bcz_v = {}
        bcze_v = {}
        qa_v = {}
        wait_led("v", "bank_PA")
        wait_led("v", "bank_PB")
        wait_led("v", "bank_PC")
        wait_led("v", "bank_PD")
        run("v", lambda e: e.memset(P01[:, :], 0.0))
        run("v", lambda e: e.memset(P23[:, :], 0.0), "SD")
        v_stinit = cnt["SD"]
        STREAMS = (
            dict(sid=0, q2=0, NB=4, ST=P01, pbase=0, zrow=M_ROW),
            dict(sid=1, q2=1, NB=8, ST=P23, pbase=2048, zrow=X_ROW),
        )
        first_st = True
        for g in range(32):
            hh = g // 4

            def a_stb(S, k):
                ST = S["ST"]
                qs = max(k - S["q2"] * 4, 0) * 128
                qbase = g * 1024 + S["q2"] * 512
                if (g, S["sid"], k - 1) in exp_v:
                    wt("t", "SA", exp_v[(g, S["sid"], k - 1)])
                elif k == 0 and (g - 1, S["sid"], S["NB"] - 1) in exp_v:
                    wt("t", "SA", exp_v[(g - 1, S["sid"], S["NB"] - 1)])
                for ti in range(2):
                    t = 2 * k + ti
                    kcol = T1B + hh * 2048 + t * 128
                    dst = ST[:, ti * 512 + qs: (ti + 1) * 512]
                    run("t", lambda e, kk=kcol, q=qbase + qs, w=512 - qs, d=dst:
                        e.matmul(d, AB[:, kk:kk + 128], AB[:, q:q + w],
                                 start=True, stop=True), "SP" if ti == 1 else None)
                return cnt["SP"]

            def a_exp(S, k, st_v):
                ST = S["ST"]
                qs = max(k - S["q2"] * 4, 0) * 128
                pb = S["pbase"] + (k % 2) * 1024
                wt("s", "SP", st_v)
                if (g, S["sid"], k - 2) in zav_v:
                    wt("s", "SP", zav_v[(g, S["sid"], k - 2)])
                elif k < 2 and (g - 1, S["sid"], S["NB"] - 1) in zav_v:
                    wt("s", "SP", zav_v[(g - 1, S["sid"], S["NB"] - 1)])
                run("s", lambda e, o=qs, p=pb: e.activation(
                    PT[:, p + o:p + 1024], ST[:, o:1024], AF.Exp,
                    scale=SCALE_ATTN), "SA")
                exp_v[(g, S["sid"], k)] = cnt["SA"]
                if k >= S["q2"] * 4:
                    wt("v", "SA", exp_v[(g, S["sid"], k)])
                    for ti in range(2):
                        t = 2 * k + ti
                        pcol = pb + ti * 512 + qs
                        run("v", lambda e, p=pcol, t_=t: e.tensor_tensor(
                            PT[:, p:p + 128], PT[:, p:p + 128],
                            MSK[:, t_ * 128:(t_ + 1) * 128], ALU.mult),
                            "SD" if ti == 1 else None)
                    return cnt["SD"]
                return None

            def a_zav(S, k, mv):
                qs = max(k - S["q2"] * 4, 0) * 128
                pb = S["pbase"] + (k % 2) * 1024
                av = PE_ if S["sid"] == 0 else PF
                zt = PG if S["sid"] == 0 else PG2
                T2 = 2 * S["NB"]
                if mv:
                    wt("t", "SD", mv)
                else:
                    wt("t", "SA", exp_v[(g, S["sid"], k)])
                if k == 0:
                    if (g - 1, S["sid"]) in bcze_v:
                        wt("t", "SA", bcze_v[(g - 1, S["sid"])])
                    elif S["sid"] == 1:
                        wait_led("t", "bank_PT8")
                    else:
                        wait_led("t", "bank_PG")
                    if (g - 1, S["sid"]) in qa_v:
                        wt("t", "SD", qa_v[(g - 1, S["sid"])])
                    else:
                        wait_led("t", "bank_PE_" if S["sid"] == 0 else "bank_PF")
                for ti in range(2):
                    t = 2 * k + ti
                    pcol = pb + ti * 512 + qs
                    pw = 512 - qs
                    s0 = (t == 0)
                    s1 = (t == T2 - 1)
                    run("t", lambda e, o=qs, p=pcol, w=pw, s0_=s0, s1_=s1, z=zt:
                        e.matmul(z[0:1, o:512], ONES, PT[:, p:p + w],
                                 start=s0_, stop=s1_))
                    vcol = KVB + t * 1024 + hh * 128
                    run("t", lambda e, v=vcol, o=qs, p=pcol, w=pw, s0_=s0, s1_=s1,
                        b=av: e.matmul(b[:, o:512], AB[:, v:v + 128], PT[:, p:p + w],
                                       start=s0_, stop=s1_),
                        "SP" if ti == 1 else None)
                return cnt["SP"]

            def a_tail(S):
                av = PE_ if S["sid"] == 0 else PF
                zt = PG if S["sid"] == 0 else PG2
                zr = S["zrow"]
                qbase = g * 1024 + S["q2"] * 512
                wt("v", "SP", zav_v[(g, S["sid"], S["NB"] - 1)])
                if (g - 1, S["sid"]) in bcz_v:
                    wt("v", "SP", bcz_v[(g - 1, S["sid"])])
                else:
                    wait_led("v", "rows")
                run("v", lambda e, z=zt, r=zr: e.reciprocal(r, z[0:1, 0:512]), "SD")
                v_zi = cnt["SD"]
                wt("t", "SD", v_zi)
                run("t", lambda e, z=zt, r=zr: e.matmul(
                    z[:, 0:512], ONEROW, r, start=True, stop=True), "SP")
                bcz_v[(g, S["sid"])] = cnt["SP"]
                wt("s", "SP", bcz_v[(g, S["sid"])])
                if (g - 1, S["sid"]) in qa_v:
                    wt("s", "SD", qa_v[(g - 1, S["sid"])])
                else:
                    wait_led("s", "US")
                uu = US[:, S["sid"] * 512:(S["sid"] + 1) * 512]
                run("s", lambda e, u=uu, z=zt: e.activation(u, z[:, 0:512], AF.Copy), "SA")
                bcze_v[(g, S["sid"])] = cnt["SA"]
                wt("v", "SA", bcze_v[(g, S["sid"])])
                run("v", lambda e, q=qbase, b=av, u=uu: e.tensor_tensor(
                    AB[:, q:q + 512], b[:, :], u, ALU.mult), "SD")
                qa_v[(g, S["sid"])] = cnt["SD"]

            if first_st:
                wt("t", "YG", v_kv_in)
                wt("t", "SD", v_stinit)
                first_st = False
            A, B = STREAMS
            mv = {}
            stv = {}
            for k in range(9):
                for S in (A, B):
                    if k < S["NB"]:
                        stv[(S["sid"], k)] = a_stb(S, k)
                for S in (A, B):
                    if 1 <= k <= S["NB"]:
                        zav_v[(g, S["sid"], k - 1)] = a_zav(S, k - 1, mv[(S["sid"], k - 1)])
                for S in (A, B):
                    if k < S["NB"]:
                        mv[(S["sid"], k)] = a_exp(S, k, stv[(S["sid"], k)])
            a_tail(A)
            a_tail(B)
        v_attn_done = qa_v[(31, 1)]
        set_led("bank_PA", "SA", exp_v[(31, 1, 7)])
        set_led("bank_PB", "SA", exp_v[(31, 1, 7)])
        set_led("bank_PC", "SA", exp_v[(31, 1, 7)])
        set_led("bank_PD", "SA", exp_v[(31, 1, 7)])
        set_led("bank_PE_", "SD", qa_v[(31, 0)])
        set_led("bank_PF", "SD", qa_v[(31, 1)])
        set_led("bank_PG", "SA", bcze_v[(31, 0)])
        set_led("bank_PT8", "SA", bcze_v[(31, 1)])
        set_led("rows", "SP", max(bcz_v[(31, 0)], bcz_v[(31, 1)]))
        set_led("US", "SD", qa_v[(31, 1)])

        if debug:
            wt("y", "SD", v_attn_done)
            dma("y", "YP", qadump.ap(), AB[:, 0:32768])

        if upto < 4:
            if debug:
                wt("y", "YP", cnt["YP"])
            wt("y", "YW0", cnt["YW0"])
            wt("y", "YW1", cnt["YW1"])
            return _finish(nc, stack, ops, block_emit)
        v_x2_done, wr_x2 = emit_oloop(wdd, lambda ich, st: ich * 1024 + st * 512, x1d, x2d,
                                      ffn=False, pre_waits=(("SD", v_attn_done),),
                                      xs_raw=v_x1_done)

        def x2_ready(grp):
            oc = grp * 4 + 3
            return (("YW0", wr_x2[2 * oc]), ("YW1", wr_x2[2 * oc + 1]))

        if upto < 5:
            wt("y", "YW0", cnt["YW0"])
            wt("y", "YW1", cnt["YW1"])
            return _finish(nc, stack, ops, block_emit)
        if debug:
            wt("v", "YP", cnt["YP"])
        emit_ffn(x2d, outd, xs_raw=v_x2_done, src_ready=x2_ready)
        wt("y", "YW0", cnt["YW0"])
        wt("y", "YW1", cnt["YW1"])
        if debug:
            wt("y", "YP", cnt["YP"])

        return _finish(nc, stack, ops, block_emit)


def block_emit(nc, ops):
    with nc.Block() as block:
        @block.tensor
        def _(e):
            for f in ops["t"]:
                f(e)

        @block.scalar
        def _(e):
            for f in ops["s"]:
                f(e)

        @block.vector
        def _(e):
            for f in ops["v"]:
                f(e)

        @block.gpsimd
        def _(e):
            for f in ops["g"]:
                f(e)

        @block.sync
        def _(e):
            for f in ops["y"]:
                f(e)


def _finish(nc, stack, ops, emit):
    try:
        emit(nc, ops)
    finally:
        stack.close()
    return nc


def _tile_lhs(w):
    O, I = w.shape
    noc, nch = O // 128, I // 128
    return np.ascontiguousarray(
        w.reshape(noc, 128, nch, 128).transpose(3, 0, 2, 1).reshape(128, noc * nch * 128))


def _tile_moving(wt):
    I, O = wt.shape
    return np.ascontiguousarray(
        wt.reshape(I // 128, 128, O).transpose(1, 0, 2).reshape(128, (I // 128) * O))


def _tile_x(tok_major):
    f = np.ascontiguousarray(tok_major.T)
    return np.ascontiguousarray(f.reshape(32, 128, 1024).transpose(1, 0, 2).reshape(128, 32768))


def _untile_x(arr):
    f = arr.reshape(128, 32, 1024).transpose(1, 0, 2).reshape(4096, 1024)
    return np.ascontiguousarray(f.T)


def _consts():
    cb = np.zeros((128, 512), np.float32)
    cb[:, 0] = 1.0
    for e in range(4):
        cb[e * 32:(e + 1) * 32, 1 + e] = 1.0
        cb[e * 32:(e + 1) * 32, 9 + 4 + e] = 1.0
    for e in range(8):
        cb[e, 17 + e * 16: 17 + (e + 1) * 16] = 1.0
    cb[:, 145:273] = np.eye(128, dtype=np.float32)
    cf = np.zeros((128, 132), np.float32)
    cf[0, 0:128] = 1.0
    cf[0, 128] = 1e-5
    return cb.astype(BF), cf


def _mask(par):
    m = np.zeros((128, 2048), np.float32)
    own = OWN[par]
    for t in range(16):
        pb = PERM_ABS[t]
        ab = own[t // 2]
        if pb < ab:
            blk = np.ones((128, 128), np.float32)
        elif pb > ab:
            blk = np.zeros((128, 128), np.float32)
        else:
            blk = np.triu(np.ones((128, 128), np.float32))
        m[:, t * 128:(t + 1) * 128] = blk
    return m.astype(BF)


def kernel(L0_attn_q, L0_attn_w_qkv, L0_attn_w_dense, L0_ffn_W, L0_ffn_b,
           L0_ffn_Wq, L0_ffn_Wk, L0_ffn_A, L0_ffn_Bm, _debug=False):
    x = np.asarray(L0_attn_q, np.float32)
    wff_t = _tile_lhs(np.asarray(L0_ffn_W, np.float32))
    wqkv_t = _tile_lhs(np.asarray(L0_attn_w_qkv, np.float32))
    wd_t = _tile_lhs(np.asarray(L0_attn_w_dense, np.float32))
    wq_t = _tile_moving(np.ascontiguousarray(np.asarray(L0_ffn_Wq, np.float32).T))
    wk_t = _tile_moving(np.ascontiguousarray(np.asarray(L0_ffn_Wk, np.float32).T))
    a2 = np.asarray(L0_ffn_A, np.float32).reshape(128, 4096)
    at_t = _tile_moving(np.ascontiguousarray(a2.T))
    bm2_t = np.ascontiguousarray(np.asarray(L0_ffn_Bm, np.float32).transpose(0, 2, 1)
                                 .reshape(128, 4096))
    b2_t = np.ascontiguousarray(np.asarray(L0_ffn_b, np.float32).reshape(32, 128).T)
    cb, cf = _consts()
    masks = [_mask(0), _mask(1)]

    in_maps = []
    for c in range(8):
        seq, par = c // 2, c % 2
        rows = np.concatenate([x[seq, ab * 128:(ab + 1) * 128, :] for ab in OWN[par]], axis=0)
        in_maps.append({
            "x": _tile_x(rows), "wff": wff_t, "wqkv": wqkv_t, "wd": wd_t,
            "wq": wq_t, "wk": wk_t, "at": at_t, "bm2": bm2_t, "b2": b2_t,
            "cb": cb, "cf": cf, "mask": masks[par],
        })

    nc = build_nc(debug=_debug)
    res = run_bass_kernel_spmd(nc, in_maps, core_ids=list(range(8)))

    y = np.empty((4, 2048, 4096), np.float32)
    for c in range(8):
        seq, par = c // 2, c % 2
        toks = _untile_x(res.results[c]["out"])
        for li, ab in enumerate(OWN[par]):
            y[seq, ab * 128:(ab + 1) * 128, :] = toks[li * 128:(li + 1) * 128]
    if _debug:
        return y, res
    return y
